# version 09
# baseline (speedup 1.0000x reference)
import numpy as np
import concourse.bass as bass
import concourse.mybir as mybir
from concourse.bass_utils import run_bass_kernel_spmd

F32 = mybir.dt.float32
BF16 = mybir.dt.bfloat16
AF = mybir.ActivationFunctionType
ALU = mybir.AluOpType
AX = mybir.AxisListType

N_CORES = 8
D = 2048
FC = D // 128
INTER = 8192
NEXP = 8
EPS = 1e-5
NH_LOC = 2
VST = 132

ENGINES = ("sync", "gpsimd", "vector", "scalar", "tensor")


class Sched:
    def __init__(self):
        self.ops = {e: [] for e in ENGINES}
        self.sems = {}
        self.counts = {}

    def wait(self, engine, sem, val):
        if val <= 0:
            return
        s = self.sems[sem]
        self.ops[engine].append(lambda eng, s=s, v=val: eng.wait_ge(s, v))

    def emit(self, engine, fn, inc=None, inck=1):
        if inc is None:
            self.ops[engine].append(fn)
            return None
        s = self.sems[inc]
        self.ops[engine].append(lambda eng, s=s, k=inck, f=fn: f(eng).then_inc(s, k))
        self.counts[inc] += inck
        return self.counts[inc]

    def val(self, sem):
        return self.counts[sem]

    def tag(self, engine, inc, inck=1):
        s = self.sems[inc]
        prev = self.ops[engine].pop()
        self.ops[engine].append(lambda eng, s=s, k=inck, f=prev: f(eng).then_inc(s, k))
        self.counts[inc] += inck
        return self.counts[inc]


def build_program(S=8192, TOK_OWN=1024, debug_outputs=False):
    NT = S // 128
    NB = S // 512
    NTO = TOK_OWN // 128
    NTB = TOK_OWN // 512
    NISL = INTER // 128

    nc = bass.Bass("TRN2", target_bir_lowering=False)

    x_d = nc.dram_tensor("x", [S, D], F32, kind="ExternalInput").ap()
    xo_d = nc.dram_tensor("x_own", [TOK_OWN, D], F32, kind="ExternalInput").ap()
    wq_d = nc.dram_tensor("wq", [D, 256], F32, kind="ExternalInput").ap()
    wk_d = nc.dram_tensor("wk", [D, 256], F32, kind="ExternalInput").ap()
    wv_d = nc.dram_tensor("wv", [D, 256], F32, kind="ExternalInput").ap()
    wo_d = nc.dram_tensor("wo", [256, D], F32, kind="ExternalInput").ap()
    bo_d = nc.dram_tensor("bo", [1, D], F32, kind="ExternalInput").ap()
    gate_d = nc.dram_tensor("gate", [D, NEXP], F32, kind="ExternalInput").ap()
    up_d = nc.dram_tensor("up", [D, INTER], F32, kind="ExternalInput").ap()
    down_d = nc.dram_tensor("down", [INTER, D], F32, kind="ExternalInput").ap()
    expw_d = nc.dram_tensor("expw", [D, D], F32, kind="ExternalInput").ap()
    gamma_d = nc.dram_tensor("gamma", [1, D], F32, kind="ExternalInput").ap()
    beta_d = nc.dram_tensor("beta", [1, D], F32, kind="ExternalInput").ap()
    out_d = nc.dram_tensor("out", [TOK_OWN, D], F32, kind="ExternalOutput").ap()
    ps_send = nc.dram_tensor("ps_send", [S, D], BF16).ap()
    ps_recv = nc.dram_tensor("ps_recv", [TOK_OWN, D], BF16).ap()
    upbf = nc.dram_tensor("upbf", [(INTER // 128) * 128, D], BF16).ap()
    downbf = nc.dram_tensor("downbf", [4 * (INTER // 512) * 128, D], BF16).ap()
    expbf = nc.dram_tensor("expbf", [4 * 128, FC * 512], BF16).ap()
    gatebf = nc.dram_tensor("gatebf", [128, FC * NEXP], BF16).ap()
    xnTo_dram = nc.dram_tensor("xnTo_dram", [(TOK_OWN // 512) * 128, FC * 512], BF16).ap()

    dbg = {}
    if debug_outputs:
        dbg["qT"] = nc.dram_tensor("dbg_qT", [128, NH_LOC * S], F32, kind="ExternalOutput").ap()
        dbg["kT"] = nc.dram_tensor("dbg_kT", [128, NH_LOC * S], F32, kind="ExternalOutput").ap()
        dbg["v"] = nc.dram_tensor("dbg_v", [128, NH_LOC * NT * VST], F32, kind="ExternalOutput").ap()
        dbg["ctx"] = nc.dram_tensor("dbg_ctx", [128, NH_LOC * S], F32, kind="ExternalOutput").ap()
        dbg["y"] = nc.dram_tensor("dbg_y", [128, NTO * D], F32, kind="ExternalOutput").ap()
        dbg["g"] = nc.dram_tensor("dbg_g", [128, NTO * D], F32, kind="ExternalOutput").ap()
        dbg["xnT"] = nc.dram_tensor("dbg_xnT", [128, FC * TOK_OWN], F32, kind="ExternalOutput").ap()

    sch = Sched()
    sem_names = [
        "s_x", "s_stats", "s_rstd", "s_xn", "s_ptr", "s_xnT",
        "s_pqk", "s_qkev", "s_pv", "s_vev", "s_wst", "s_msk",
        "s_S", "s_P", "s_Pm", "s_AV", "s_ctx",
        "s_ctxT", "s_ctxTe", "s_wops", "s_psst", "s_psdma", "s_coll",
        "s_gdma", "s_glog", "s_scl",
        "s_ownx", "s_ost", "s_orstd", "s_oxn", "s_optr", "s_oxnT",
        "s_exdma", "s_ey", "s_yev",
        "s_updma", "s_up", "s_hrel", "s_h2",
        "s_dwdma", "s_dwn", "s_dev",
        "s_gb", "s_lnst", "s_lnr", "s_yn", "s_g",
        "s_sdma", "s_xodma", "s_fin", "s_out", "s_dbg", "s_pcast", "s_jA", "s_jB",
        "s_x0", "s_x1", "s_x2", "s_ox0", "s_ox1", "s_ox2",
        "s_upd0", "s_upd1", "s_dwd0", "s_dwd1", "s_exd0", "s_exd1",
        "s_sd0", "s_sd1", "s_xo0", "s_xo1", "s_ps0", "s_ps1",
    ]

    open_cms = []

    def close_all():
        for cm in reversed(open_cms):
            cm.__exit__(None, None, None)

    def mk(cm):
        t = cm.__enter__()
        open_cms.append(cm)
        return t, cm

    def mk_sb(name, shape, dtype):
        return mk(nc.sbuf_tensor(name, shape, dtype))

    def mk_ps(name, shape, dtype):
        return mk(nc.psum_tensor(name, shape, dtype))

    def free(cms):
        for cm in reversed(cms):
            cm.__exit__(None, None, None)
            open_cms.remove(cm)

    for s in sem_names:
        h, _ = mk(nc.semaphore(s))
        sch.sems[s] = h
        sch.counts[s] = 0

    E = sch.emit
    W = sch.wait

    ident, _ = mk_sb("ident", [128, 128], BF16)
    trimask, _ = mk_sb("trimask", [128, 128], BF16)
    ones_f32, _ = mk_sb("ones_f32", [1, 128], F32)
    NTT = NT + NTO
    stats, _ = mk_sb("stats", [128, 8 * NTT], F32)

    def stc(kind, t, w=1):
        i = {"sx": 0, "sq": 1, "negmu": 2, "m2e": 3, "musq": 4,
             "vare": 5, "lnv": 6, "rstd": 7}[kind]
        return stats[:, i * NTT + t:i * NTT + t + w]

    def p0(g):
        g.memset(ident[:], 1.0)
        g.drain()
        g.affine_select(out=ident[:], in_=ident[:], pattern=[[-1, 128]],
                        compare_op=ALU.is_equal, fill=0.0, base=0, channel_multiplier=1)
        g.memset(trimask[:], 1.0)
        g.drain()
        g.affine_select(out=trimask[:], in_=trimask[:], pattern=[[1, 128]],
                        compare_op=ALU.is_ge, fill=0.0, base=0, channel_multiplier=-1)
        return g.memset(ones_f32[:], 1.0)

    E("gpsimd", p0, inc="s_msk")

    def emit_ln_accum(xb, t, jk, wait_sem, wait_val):
        W("vector", wait_sem, wait_val)
        W("vector", "s_jA", sch.val("s_jA"))
        E("vector", lambda v, xb=xb, t=t, jk=jk: v.tensor_scalar(
            out=jk[:], in0=xb[:], scalar1=1.0, scalar2=0.0, op0=ALU.mult, op1=ALU.add,
            accum_out=stc("sx", t)), inc="s_jA")
        W("vector", "s_jA", sch.val("s_jA"))
        E("vector", lambda v, xb=xb, t=t, jk=jk: v.scalar_tensor_tensor(
            out=jk[:], in0=xb[:], scalar=1.0, in1=xb[:], op0=ALU.mult, op1=ALU.mult,
            accum_out=stc("sq", t)), inc="s_jA")

    def emit_ln_block(t0, w, sem_stats, sem_musq, sem_vare, sem_rstd):
        E("vector", lambda v: v.drain())
        E("vector", lambda v, t0=t0, w=w: v.tensor_scalar(
            out=stc("negmu", t0, w), in0=stc("sx", t0, w), scalar1=-1.0 / D,
            scalar2=None, op0=ALU.mult))
        E("vector", lambda v, t0=t0, w=w: v.tensor_scalar(
            out=stc("m2e", t0, w), in0=stc("sq", t0, w), scalar1=1.0 / D, scalar2=EPS,
            op0=ALU.mult, op1=ALU.add), inc=sem_stats)
        W("scalar", sem_stats, sch.val(sem_stats))
        E("scalar", lambda s, t0=t0, w=w: s.activation(
            out=stc("musq", t0, w), in_=stc("negmu", t0, w), func=AF.Square), inc=sem_musq)
        W("vector", sem_musq, sch.val(sem_musq))
        E("vector", lambda v, t0=t0, w=w: v.tensor_tensor(
            out=stc("vare", t0, w), in0=stc("m2e", t0, w), in1=stc("musq", t0, w),
            op=ALU.subtract), inc=sem_vare)
        W("scalar", sem_vare, sch.val(sem_vare))
        E("scalar", lambda s, t0=t0, w=w: s.activation(
            out=stc("lnv", t0, w), in_=stc("vare", t0, w), func=AF.Ln))
        E("scalar", lambda s: s.drain())
        E("scalar", lambda s, t0=t0, w=w: s.activation(
            out=stc("rstd", t0, w), in_=stc("lnv", t0, w), func=AF.Exp, scale=-0.5),
          inc=sem_rstd)

    def emit_ln_norm(xb, t, sem_rstd, sem_xn):
        W("vector", sem_rstd, sch.val(sem_rstd))
        E("vector", lambda v, xb=xb, t=t: v.tensor_scalar(
            out=xb[:], in0=xb[:], scalar1=stc("negmu", t), scalar2=stc("rstd", t),
            op0=ALU.add, op1=ALU.mult), inc=sem_xn)

    def emit_transpose16(xb, t, banks, sem_ptr, sem_evac, dst_fn, wait_sem):
        W("tensor", wait_sem, sch.val(wait_sem))
        for half in range(2):
            bank = banks[half]
            if t >= 1:
                W("tensor", sem_evac, (t - 1) * 2 + half + 1)
            for j in range(8):
                c = half * 8 + j
                E("tensor", lambda pe, xb=xb, bank=bank, j=j, c=c: pe.transpose(
                    out=bank[:, j * 128:(j + 1) * 128], in_=xb[:, c * 128:(c + 1) * 128],
                    identity=ident[:]))
            sch.tag("tensor", sem_ptr)
        for half in range(2):
            W("vector", sem_ptr, sch.val(sem_ptr) - (1 - half))
            dst = dst_fn(half)
            src = banks[half][:].rearrange("p (c q) -> p c q", c=8)
            E("vector", lambda v, dst=dst, src=src: v.tensor_copy(out=dst, in_=src), inc=sem_evac)

    b_cms = []

    def mk_sbB(name, shape, dtype):
        t, cm = mk_sb(name, shape, dtype)
        b_cms.append(cm)
        return t

    qT = mk_sbB("qT", [128, NH_LOC * S], BF16)
    kT = mk_sbB("kT", [128, NH_LOC * S], BF16)
    vb = mk_sbB("vb", [128, NH_LOC * NT * VST], BF16)

    b1_cms = []

    def mk_sbB1(name, shape, dtype):
        t, cm = mk_sb(name, shape, dtype)
        b1_cms.append(cm)
        return t

    wqk_st = mk_sbB1("wqk_st", [128, FC * 512], BF16)
    wv_st = mk_sbB1("wv_st", [128, FC * 256], BF16)
    xbf = [mk_sbB1(f"xbf{i}", [128, D], BF16) for i in range(6)]
    x32 = [mk_sbB1(f"x32_{i}", [128, D], F32) for i in range(2)]
    junk = mk_sbB1("junkb", [128, D], BF16)
    xnT = [mk_sbB1(f"xnT{i}", [128, FC * 512], BF16) for i in range(2)]

    ps1_cms = []

    def mk_psS1(name, shape, dtype):
        t, cm = mk_ps(name, shape, dtype)
        ps1_cms.append(cm)
        return t

    ptrb = [mk_psS1(f"ptrb{i}", [128, 1024], BF16) for i in range(2)]
    pqk = [mk_psS1(f"pqk{i}", [128, 512], F32) for i in range(4)]
    pv = [mk_psS1(f"pv{i}", [128, 256], F32) for i in range(2)]

    E("vector", lambda v: v.memset(vb[:], 1.0))
    E("vector", lambda v: v.drain())

    def wdma(fn):
        E("gpsimd", fn, inc="s_wst", inck=16)

    wqk_v = wqk_st[:].rearrange("p (c z) -> p c z", c=FC)
    for part in range(4):
        lo, hi = part * 4, part * 4 + 4
        wdma(lambda g, lo=lo, hi=hi: g.dma_start(
            out=wqk_v[:, lo:hi, 0:256],
            in_=wq_d.rearrange("(c p) q -> p c q", p=128)[:, lo:hi, :]))
        wdma(lambda g, lo=lo, hi=hi: g.dma_start(
            out=wqk_v[:, lo:hi, 256:512],
            in_=wk_d.rearrange("(c p) q -> p c q", p=128)[:, lo:hi, :]))
        wdma(lambda g, lo=lo, hi=hi: g.dma_start(
            out=wv_st[:].rearrange("p (c q) -> p c q", c=FC)[:, lo:hi, :],
            in_=wv_d.rearrange("(c p) q -> p c q", p=128)[:, lo:hi, :]))
    WST_ALL = sch.val("s_wst")

    XSLOT = ["s_x0", "s_x1"]
    for t in range(NT + NTO):
        b, ti = t // 4, t % 4
        xb = xbf[t % 6]
        xs = x32[t % 2]
        if t >= 2:
            W("sync", "s_ownx", t - 1)
        if t < NT:
            E("sync", lambda s, xs=xs, t=t: s.dma_start(
                out=xs[:], in_=x_d[t * 128:(t + 1) * 128, :]), inc=XSLOT[t % 2], inck=16)
        else:
            to = t - NT
            E("sync", lambda s, xs=xs, to=to: s.dma_start(
                out=xs[:], in_=xo_d[to * 128:(to + 1) * 128, :]), inc=XSLOT[t % 2], inck=16)
        W("scalar", XSLOT[t % 2], 16 * (t // 2 + 1))
        if t >= 6:
            W("scalar", "s_ptr", 2 * (t - 5))
        E("scalar", lambda s, xb=xb, xs=xs: s.copy(out=xb[:], in_=xs[:]), inc="s_ownx")
        emit_ln_accum(xb, t, junk, "s_ownx", sch.val("s_ownx"))
        if t == 0:
            W("tensor", "s_msk", 1)
            W("tensor", "s_wst", WST_ALL)
        if ti != 3:
            continue
        emit_ln_block(t - 3, 4, "s_stats", "s_ost", "s_orstd", "s_rstd")
        for tt in range(4):
            emit_ln_norm(xbf[(t - 3 + tt) % 6], t - 3 + tt, "s_rstd", "s_xn")
        xt = xnT[b % 2]
        if b >= NB and b - NB >= 2:
            W("vector", "s_oxnT", 16 * (b - NB - 1))
        for tt in range(4):
            tg = t - 3 + tt
            dst_fn = (lambda half, xt=xt, tt=tt: xt[:].rearrange("p (c q) -> p c q", c=FC)[
                :, half * 8:(half + 1) * 8, tt * 128:(tt + 1) * 128])
            emit_transpose16(xbf[tg % 6], tg, ptrb, "s_ptr", "s_xnT", dst_fn, "s_xn")
        if t >= NT:
            bo_ = b - NB
            W("sync", "s_xnT", sch.val("s_xnT"))
            E("sync", lambda s, bo_=bo_, xt=xt: s.dma_start(
                out=xnTo_dram[bo_ * 128:(bo_ + 1) * 128, :], in_=xt[:]),
              inc=f"s_ox{bo_}", inck=16)

        if t < NT:
            W("tensor", "s_xnT", sch.val("s_xnT"))
            if b >= 1:
                W("tensor", "s_qkev", b)
            for sl in range(4):
                for c in range(FC):
                    E("tensor", lambda pe, sl=sl, c=c, xt=xt: pe.matmul(
                        out=pqk[sl][:],
                        lhsT=wqk_st[:, c * 512 + sl * 128:c * 512 + (sl + 1) * 128],
                        rhs=xt[:, c * 512:(c + 1) * 512],
                        start=(c == 0), stop=(c == FC - 1)))
            sch.tag("tensor", "s_pqk")
            for tt in range(4):
                kt_g = b * 4 + tt
                if kt_g >= 2:
                    W("tensor", "s_vev", kt_g - 1)
                for c in range(FC):
                    E("tensor", lambda pe, tt=tt, c=c, xt=xt: pe.matmul(
                        out=pv[tt % 2][:],
                        lhsT=xt[:, c * 512 + tt * 128:c * 512 + (tt + 1) * 128],
                        rhs=wv_st[:, c * 256:(c + 1) * 256],
                        start=(c == 0), stop=(c == FC - 1)))
                sch.tag("tensor", "s_pv")
            W("vector", "s_pqk", sch.val("s_pqk"))
            for sl in range(4):
                tgt = qT if sl < 2 else kT
                h = sl % 2
                E("vector", lambda v, sl=sl, tgt=tgt, h=h, b=b: v.tensor_copy(
                    out=tgt[:, h * S + b * 512:h * S + (b + 1) * 512], in_=pqk[sl][:]))
            sch.tag("vector", "s_qkev")
            for tt in range(4):
                kt_g = b * 4 + tt
                W("vector", "s_pv", sch.val("s_pv") - (3 - tt))
                for h in range(NH_LOC):
                    E("vector", lambda v, tt=tt, h=h, kt_g=kt_g: v.tensor_copy(
                        out=vb[:, h * NT * VST + kt_g * VST:h * NT * VST + kt_g * VST + 128],
                        in_=pv[tt % 2][:, h * 128:(h + 1) * 128]))
                sch.tag("vector", "s_vev")

    P1_XNT_ALL = sch.val("s_xnT")

    if debug_outputs:
        W("gpsimd", "s_qkev", sch.val("s_qkev"))
        W("gpsimd", "s_vev", sch.val("s_vev"))
        E("gpsimd", lambda s: s.dma_start(out=dbg["qT"], in_=qT[:]), inc="s_dbg", inck=16)
        E("gpsimd", lambda s: s.dma_start(out=dbg["kT"], in_=kT[:]), inc="s_dbg", inck=16)
        E("gpsimd", lambda s: s.dma_start(out=dbg["v"], in_=vb[:]), inc="s_dbg", inck=16)

    free(ps1_cms[:])
    free(b1_cms[:])

    b2_cms = []

    def mk_sbB2(name, shape, dtype):
        t, cm = mk_sb(name, shape, dtype)
        b2_cms.append(cm)
        return t

    ctxb = mk_sbB2("ctxb", [128, NH_LOC * S], BF16)
    wo_st = mk_sbB2("wo_st", [128, 2 * D], BF16)
    W("gpsimd", "s_vev", sch.val("s_vev"))
    W("gpsimd", "s_pv", sch.val("s_pv"))
    W("gpsimd", "s_xnT", sch.val("s_xnT"))
    E("gpsimd", lambda g: g.dma_start(
        out=wo_st[:].rearrange("p (c q) -> p c q", c=2),
        in_=wo_d.rearrange("(c p) q -> p c q", p=128)), inc="s_wst", inck=16)
    upbf_v = upbf.rearrange("(s p) x -> s p x", p=128)
    for isl in range(INTER // 128):
        E("gpsimd", lambda g, isl=isl: g.dma_start(
            out=upbf_v[isl].rearrange("p (c i) -> p c i", i=128),
            in_=up_d.rearrange("(c p) (s i) -> s p c i", p=128, i=128)[isl]),
          inc="s_pcast", inck=16)
    dnbf_v = downbf.rearrange("(f g p) x -> f g p x", g=INTER // 512, p=128)
    for fb in range(4):
        for grp in range(INTER // 512):
            E("gpsimd", lambda g, fb=fb, grp=grp: g.dma_start(
                out=dnbf_v[fb, grp].rearrange("p (c q) -> p c q", c=4),
                in_=down_d.rearrange("(g c p) q -> g p c q", p=128, c=4)[grp][:, :,
                                                                             fb * 512:(fb + 1) * 512]),
              inc="s_pcast", inck=16)
    exbf_v = expbf.rearrange("(f p) x -> f p x", p=128)
    for fb in range(4):
        for half in range(2):
            E("gpsimd", lambda g, fb=fb, half=half: g.dma_start(
                out=exbf_v[fb].rearrange("p (c q) -> p c q", c=16)[:, half * 8:(half + 1) * 8, :],
                in_=expw_d.rearrange("(c p) q -> p c q", p=128)[
                    :, half * 8:(half + 1) * 8, fb * 512:(fb + 1) * 512]),
              inc="s_pcast", inck=16)
    E("gpsimd", lambda g: g.dma_start(
        out=gatebf.rearrange("p (c e) -> p c e", c=FC),
        in_=gate_d.rearrange("(c p) e -> p c e", p=128)), inc="s_pcast", inck=16)
    PCAST_ALL = sch.val("s_pcast")
    pTb = [mk_sbB2(f"pT{i}", [128, 1536], BF16) for i in range(2)]
    pdiag = mk_sbB2("pdiag", [128, 4 * 128], BF16)
    rc4 = mk_sbB2("rc4", [128, 4], F32)
    ctxT_sb = [mk_sbB2(f"ctxT{i}", [128, 256], BF16) for i in range(2)]
    ps_stage = [mk_sbB2(f"psst{i}", [128, D], BF16) for i in range(2)]

    ps2_cms = []
    pS = []
    for i in range(2):
        t_, cm = mk_ps(f"pS{i}", [128, 1536], F32)
        ps2_cms.append(cm)
        pS.append(t_)
    pC = []
    for i in range(2):
        t_, cm = mk_ps(f"pC{i}", [128, 512], F32)
        ps2_cms.append(cm)
        pC.append(t_)

    def ctx_ps(qt):
        return pC[qt // 2][:, (qt % 2) * 256:(qt % 2) * 256 + 129]

    W("tensor", "s_qkev", sch.val("s_qkev"))
    W("tensor", "s_vev", sch.val("s_vev"))
    W("tensor", "s_xnT", P1_XNT_ALL)

    for h in range(NH_LOC):
        hq, hv = h * S, h * NT * VST
        for qb in range(NB):
            nk = 4 * (qb + 1)
            ngroups = (nk + 2) // 3
            p_cnt = {}
            av_cnt = {}
            pm_cnt = {}

            def emit_S(g):
                j0, j1 = 3 * g, min(3 * g + 3, nk)
                bank = pS[g % 2]
                if g >= 2:
                    W("tensor", "s_P", p_cnt[g - 2])
                if g == 0:
                    W("tensor", "s_ctx", sch.val("s_ctx"))
                for j in range(j0, j1):
                    E("tensor", lambda pe, j=j, j0=j0, bank=bank, hq=hq, qb=qb: pe.matmul(
                        out=bank[:, (j - j0) * 512:(j - j0 + 1) * 512],
                        lhsT=kT[:, hq + j * 128:hq + (j + 1) * 128],
                        rhs=qT[:, hq + qb * 512:hq + (qb + 1) * 512],
                        start=True, stop=True))
                sch.tag("tensor", "s_S")

            def emit_exp(g):
                j0, j1 = 3 * g, min(3 * g + 3, nk)
                n = (j1 - j0) * 512
                W("scalar", "s_S", sch.val("s_S"))
                if g >= 2:
                    W("scalar", "s_AV", av_cnt[g - 2])
                E("scalar", lambda s, g=g, n=n: s.activation(
                    out=pTb[g % 2][:, 0:n], in_=pS[g % 2][:, 0:n], func=AF.Exp), inc="s_P")
                p_cnt[g] = sch.val("s_P")
                dt_ = [(j, j - 4 * qb) for j in range(j0, j1) if j >= 4 * qb]
                for (j, qt) in dt_:
                    W("vector", "s_P", p_cnt[g])
                    E("vector", lambda v, g=g, j=j, j0=j0, qt=qt: v.tensor_tensor(
                        out=pdiag[:, qt * 128:(qt + 1) * 128],
                        in0=pTb[g % 2][:, (j - j0) * 512 + qt * 128:(j - j0) * 512 + (qt + 1) * 128],
                        in1=trimask[:], op=ALU.mult), inc="s_Pm")
                pm_cnt[g] = sch.val("s_Pm")

            def emit_AV(g):
                j0, j1 = 3 * g, min(3 * g + 3, nk)
                W("tensor", "s_P", p_cnt[g])
                if pm_cnt.get(g, 0):
                    W("tensor", "s_Pm", pm_cnt[g])
                for j in range(j0, j1):
                    for qt in range(4):
                        jmax = 4 * qb + qt
                        if j > jmax:
                            continue
                        if j == jmax:
                            lhs = pdiag[:, qt * 128:(qt + 1) * 128]
                        else:
                            lhs = pTb[g % 2][:, (j - j0) * 512 + qt * 128:(j - j0) * 512 + (qt + 1) * 128]
                        E("tensor", lambda pe, lhs=lhs, qt=qt, j=j, jmax=jmax, hv=hv: pe.matmul(
                            out=ctx_ps(qt), lhsT=lhs,
                            rhs=vb[:, hv + j * VST:hv + j * VST + 129],
                            start=(j == 0 and qt % 2 == 0),
                            stop=(j == jmax and qt % 2 == 1)))
                sch.tag("tensor", "s_AV")
                av_cnt[g] = sch.val("s_AV")

            for g in range(ngroups):
                emit_S(g)
                emit_exp(g)
                if g >= 1:
                    emit_AV(g - 1)
            emit_AV(ngroups - 1)

            W("vector", "s_AV", av_cnt[ngroups - 1])
            for qt in range(4):
                E("vector", lambda v, qt=qt: v.reciprocal(
                    out=rc4[:, qt:qt + 1], in_=ctx_ps(qt)[:, 128:129]))
            E("vector", lambda v: v.drain())
            for qt in range(4):
                tix = qb * 4 + qt
                E("vector", lambda v, qt=qt, tix=tix, hq=hq: v.tensor_scalar(
                    out=ctxb[:, hq + tix * 128:hq + (tix + 1) * 128],
                    in0=ctx_ps(qt)[:, 0:128],
                    scalar1=rc4[:, qt:qt + 1], scalar2=None, op0=ALU.mult))
            sch.tag("vector", "s_ctx")

    if debug_outputs:
        W("gpsimd", "s_ctx", sch.val("s_ctx"))
        E("gpsimd", lambda s: s.dma_start(out=dbg["ctx"], in_=ctxb[:]), inc="s_dbg", inck=16)

    free(ps2_cms[:])

    ps25_cms = []
    pCt = []
    for i in range(2):
        t_, cm = mk_ps(f"pCt{i}", [128, 256], BF16)
        ps25_cms.append(cm)
        pCt.append(t_)
    pWo = []
    for i in range(4):
        t_, cm = mk_ps(f"pWo{i}", [128, 512], F32)
        ps25_cms.append(cm)
        pWo.append(t_)

    W("tensor", "s_wst", sch.val("s_wst"))
    for t in range(NT):
        if t >= 2:
            W("tensor", "s_ctxTe", t - 1)
        for h in range(NH_LOC):
            E("tensor", lambda pe, t=t, h=h: pe.transpose(
                out=pCt[t % 2][:, h * 128:(h + 1) * 128],
                in_=ctxb[:, h * S + t * 128:h * S + (t + 1) * 128],
                identity=ident[:]))
        sch.tag("tensor", "s_ctxT")
        W("vector", "s_ctxT", sch.val("s_ctxT"))
        E("vector", lambda v, t=t: v.tensor_copy(
            out=ctxT_sb[t % 2][:], in_=pCt[t % 2][:]), inc="s_ctxTe")
        W("tensor", "s_ctxTe", sch.val("s_ctxTe"))
        for fb in range(4):
            if t >= 1:
                W("tensor", "s_psst", (t - 1) * 4 + fb + 1)
            for ch in range(2):
                E("tensor", lambda pe, t=t, fb=fb, ch=ch: pe.matmul(
                    out=pWo[fb][:],
                    lhsT=ctxT_sb[t % 2][:, ch * 128:(ch + 1) * 128],
                    rhs=wo_st[:, ch * D + fb * 512:ch * D + (fb + 1) * 512],
                    start=(ch == 0), stop=(ch == 1)))
            sch.tag("tensor", "s_wops")
            W("scalar", "s_wops", sch.val("s_wops"))
            if fb == 0 and t >= 2:
                W("scalar", f"s_ps{t % 2}", 16 * (t // 2))
            E("scalar", lambda s, t=t, fb=fb: s.copy(
                out=ps_stage[t % 2][:, fb * 512:(fb + 1) * 512], in_=pWo[fb][:]), inc="s_psst")
        W("sync", "s_psst", sch.val("s_psst"))
        E("sync", lambda s, t=t: s.dma_start(
            out=ps_send[t * 128:(t + 1) * 128, :], in_=ps_stage[t % 2][:]), inc=f"s_ps{t % 2}", inck=16)

    W("gpsimd", "s_ps0", 16 * (NT // 2))
    W("gpsimd", "s_ps1", 16 * (NT // 2))
    E("gpsimd", lambda g: g.collective_compute(
        "ReduceScatter", ALU.add, replica_groups=[list(range(N_CORES))],
        ins=[ps_send.opt()], outs=[ps_recv.opt()]), inc="s_coll")

    free(ps25_cms[:])
    free(b2_cms[:])
    free(b_cms[:])

    f_cms = []

    def mk_sbF(name, shape, dtype):
        t, cm = mk_sb(name, shape, dtype)
        f_cms.append(cm)
        return t

    gate_st = mk_sbF("gate_st", [128, FC * NEXP], BF16)
    y_sb = mk_sbF("y_sb", [128, NTO * D], BF16)
    logits = mk_sbF("logits", [128, NTO * NEXP], F32)
    l_tmp = mk_sbF("l_tmp", [128, NEXP], F32)
    max12 = mk_sbF("max12", [128, 2], F32)
    rscl = mk_sbF("rscl", [128, NTO], F32)

    W("sync", "s_wops", sch.val("s_wops"))
    W("sync", "s_ps0", 16 * (NT // 2))
    W("sync", "s_ps1", 16 * (NT // 2))
    W("sync", "s_pcast", PCAST_ALL)
    if debug_outputs:
        W("sync", "s_dbg", sch.val("s_dbg"))
    E("sync", lambda s: s.dma_start(out=gate_st[:], in_=gatebf), inc="s_gdma", inck=16)

    xnTo = mk_sbF("xnTo_sb", [128, FC * TOK_OWN], BF16)
    for bo in range(TOK_OWN // 512):
        W("sync", f"s_ox{bo}", 16)
        E("sync", lambda s, bo=bo: s.dma_start(
            out=xnTo[:].rearrange("p (c q) -> p c q", c=FC)[:, :, bo * 512:(bo + 1) * 512],
            in_=xnTo_dram[bo * 128:(bo + 1) * 128, :].rearrange("p (c q) -> p c q", c=FC)),
          inc="s_gdma", inck=16)

    if debug_outputs:
        W("gpsimd", "s_gdma", sch.val("s_gdma"))
        E("gpsimd", lambda s: s.dma_start(out=dbg["xnT"], in_=xnTo[:]), inc="s_dbg", inck=16)

    psF_cms = []
    pY = []
    for i in range(8):
        t_, cm = mk_ps(f"pY{i}", [128, 512], F32)
        psF_cms.append(cm)
        pY.append(t_)

    W("tensor", "s_xnT", P1_XNT_ALL)
    W("tensor", "s_gdma", sch.val("s_gdma"))
    for t in range(NTO):
        if t >= 1:
            W("tensor", "s_glog", t)
        for c in range(FC):
            E("tensor", lambda pe, t=t, c=c: pe.matmul(
                out=pY[0][:, 0:NEXP],
                lhsT=xnTo[:, c * TOK_OWN + t * 128:c * TOK_OWN + (t + 1) * 128],
                rhs=gate_st[:, c * NEXP:(c + 1) * NEXP],
                start=(c == 0), stop=(c == FC - 1)))
        sch.tag("tensor", "s_S")
        W("vector", "s_S", sch.val("s_S"))
        E("vector", lambda v, t=t: v.tensor_copy(
            out=logits[:, t * NEXP:(t + 1) * NEXP], in_=pY[0][:, 0:NEXP]), inc="s_glog")

    for t in range(NTO):
        lg = logits[:, t * NEXP:(t + 1) * NEXP]
        E("vector", lambda v, lg=lg: v.tensor_reduce(
            out=max12[:, 0:1], in_=lg, axis=AX.X, op=ALU.max))
        E("vector", lambda v: v.drain())
        E("vector", lambda v, lg=lg: v.tensor_scalar(
            out=l_tmp[:], in0=lg, scalar1=max12[:, 0:1], scalar2=None, op0=ALU.is_equal))
        E("vector", lambda v: v.drain())
        E("vector", lambda v, lg=lg: v.scalar_tensor_tensor(
            out=l_tmp[:], in0=l_tmp[:], scalar=-3.0e38, in1=lg, op0=ALU.mult, op1=ALU.add))
        E("vector", lambda v: v.drain())
        E("vector", lambda v: v.tensor_reduce(
            out=max12[:, 1:2], in_=l_tmp[:], axis=AX.X, op=ALU.max))
        E("vector", lambda v: v.drain())
        E("vector", lambda v, t=t: v.tensor_tensor(
            out=rscl[:, t:t + 1], in0=max12[:, 0:1], in1=max12[:, 1:2], op=ALU.add))
        E("vector", lambda v: v.drain())
    sch.tag("vector", "s_scl")

    ex_cms = []

    def mk_sbEX(name, shape, dtype):
        t, cm = mk_sb(name, shape, dtype)
        ex_cms.append(cm)
        return t

    exslice = [mk_sbEX(f"exsl{i}", [128, FC * 512], BF16) for i in range(2)]
    for fb in range(4):
        if fb >= 2:
            W("sync", "s_ey", fb - 1)
        E("sync", lambda s, fb=fb: s.dma_start(
            out=exslice[fb % 2][:], in_=expbf[fb * 128:(fb + 1) * 128, :]),
          inc=f"s_exd{fb % 2}", inck=16)
        W("tensor", f"s_exd{fb % 2}", 16 * (fb // 2 + 1))
        if fb == 0:
            W("tensor", "s_glog", NTO)
        if fb >= 1:
            W("tensor", "s_yev", fb)
        for c in range(FC):
            for T in range(NTO):
                E("tensor", lambda pe, c=c, T=T, fb=fb: pe.matmul(
                    out=pY[T][:],
                    lhsT=xnTo[:, c * TOK_OWN + T * 128:c * TOK_OWN + (T + 1) * 128],
                    rhs=exslice[fb % 2][:, c * 512:(c + 1) * 512],
                    start=(c == 0), stop=(c == FC - 1)))
        sch.tag("tensor", "s_ey")
        W("vector", "s_ey", sch.val("s_ey"))
        if fb == 0:
            W("vector", "s_scl", 1)
        for T in range(NTO):
            E("vector", lambda v, T=T, fb=fb: v.tensor_scalar(
                out=y_sb[:, T * D + fb * 512:T * D + (fb + 1) * 512],
                in0=pY[T][:], scalar1=rscl[:, T:T + 1], scalar2=None, op0=ALU.mult))
        sch.tag("vector", "s_yev")

    free(ex_cms[:])
    ffnw_cms = []

    def mk_sbFW(name, shape, dtype):
        t, cm = mk_sb(name, shape, dtype)
        ffnw_cms.append(cm)
        return t

    h2T = mk_sbFW("h2T", [128, NISL * TOK_OWN], BF16)
    up_cms = []

    def mk_sbUP(name, shape, dtype):
        t, cm = mk_sb(name, shape, dtype)
        up_cms.append(cm)
        return t

    upst = [mk_sbUP(f"upst{i}", [128, D], BF16) for i in range(2)]
    hrel = [mk_sbUP(f"hrel{i}", [128, TOK_OWN], BF16) for i in range(2)]
    pH = [pY[0], pY[1]]

    for isl in range(NISL):
        ub = upst[isl % 2]
        if isl == 0:
            W("sync", "s_ey", 4)
        if isl >= 2:
            W("sync", "s_up", isl - 1)
        E("sync", lambda s, isl=isl, ub=ub: s.dma_start(
            out=ub[:], in_=upbf[isl * 128:(isl + 1) * 128, :]),
          inc=f"s_upd{isl % 2}", inck=16)
        W("tensor", f"s_upd{isl % 2}", 16 * (isl // 2 + 1))
        if isl == 0:
            W("tensor", "s_yev", 4)
        if isl >= 1:
            W("tensor", "s_hrel", isl)
        for c in range(FC):
            for tb in range(NTB):
                E("tensor", lambda pe, c=c, tb=tb, ub=ub: pe.matmul(
                    out=pH[tb][:],
                    lhsT=ub[:, c * 128:(c + 1) * 128],
                    rhs=xnTo[:, c * TOK_OWN + tb * 512:c * TOK_OWN + (tb + 1) * 512],
                    start=(c == 0), stop=(c == FC - 1)))
        sch.tag("tensor", "s_up")
        W("scalar", "s_up", sch.val("s_up"))
        if isl >= 2:
            W("scalar", "s_h2", isl - 1)
        for tb in range(NTB):
            E("scalar", lambda s, tb=tb, isl=isl: s.activation(
                out=hrel[isl % 2][:, tb * 512:(tb + 1) * 512], in_=pH[tb][:], func=AF.Relu))
        sch.tag("scalar", "s_hrel")
        W("vector", "s_hrel", sch.val("s_hrel"))
        E("vector", lambda v, isl=isl: v.tensor_tensor(
            out=h2T[:, isl * TOK_OWN:(isl + 1) * TOK_OWN],
            in0=hrel[isl % 2][:], in1=hrel[isl % 2][:], op=ALU.mult), inc="s_h2")

    free(up_cms[:])
    dwst = [mk_sbFW(f"dwst{i}", [128, 4 * 512], BF16) for i in range(2)]
    NGRP = NISL // 4
    W("tensor", "s_h2", sch.val("s_h2"))
    for fb in range(4):
        for grp in range(NGRP):
            gidx = fb * NGRP + grp
            if gidx < 2:
                W("sync", "s_up", NISL)
            if gidx >= 2:
                W("sync", "s_dwn", gidx - 1)
            E("sync", lambda s, fb=fb, grp=grp, gidx=gidx: s.dma_start(
                out=dwst[gidx % 2][:],
                in_=downbf[(fb * NGRP + grp) * 128:(fb * NGRP + grp + 1) * 128, :]),
              inc=f"s_dwd{gidx % 2}", inck=16)
            W("tensor", f"s_dwd{gidx % 2}", 16 * (gidx // 2 + 1))
            if grp == 0 and fb >= 1:
                W("tensor", "s_dev", fb)
            for i8 in range(4):
                isl = grp * 4 + i8
                for T in range(NTO):
                    E("tensor", lambda pe, gidx=gidx, i8=i8, isl=isl, T=T: pe.matmul(
                        out=pY[T][:],
                        lhsT=h2T[:, isl * TOK_OWN + T * 128:isl * TOK_OWN + (T + 1) * 128],
                        rhs=dwst[gidx % 2][:, i8 * 512:(i8 + 1) * 512],
                        start=(isl == 0), stop=(isl == NISL - 1)))
            sch.tag("tensor", "s_dwn")
        W("vector", "s_dwn", sch.val("s_dwn"))
        for T in range(NTO):
            yap = y_sb[:, T * D + fb * 512:T * D + (fb + 1) * 512]
            E("vector", lambda v, T=T, yap=yap: v.tensor_tensor(
                out=yap, in0=pY[T][:], in1=yap, op=ALU.add))
        sch.tag("vector", "s_dev")

    if debug_outputs:
        W("gpsimd", "s_dev", sch.val("s_dev"))
        E("gpsimd", lambda s: s.dma_start(out=dbg["y"], in_=y_sb[:]), inc="s_dbg", inck=16)


    free(ffnw_cms[:])

    W("sync", "s_dev", sch.val("s_dev"))

    junk3 = mk_sbF("junk3A", [128, D], BF16)
    gb_row = mk_sbF("gb_row", [1, 3 * D], F32)
    gam_bc = mk_sbF("gam_bc", [128, D], BF16)
    bet_bc = mk_sbF("bet_bc", [128, D], BF16)
    bo_bc = mk_sbF("bo_bc", [128, D], F32)
    yn_t = [mk_sbF(f"yn{i}", [128, D], BF16) for i in range(2)]
    g_sb = mk_sbF("g_sb", [128, NTO * D], BF16)

    E("sync", lambda s: s.dma_start(out=gb_row[0:1, 0:D], in_=gamma_d), inc="s_gdma", inck=16)
    E("sync", lambda s: s.dma_start(out=gb_row[0:1, D:2 * D], in_=beta_d), inc="s_gdma", inck=16)
    E("sync", lambda s: s.dma_start(out=gb_row[0:1, 2 * D:3 * D], in_=bo_d), inc="s_gdma", inck=16)
    W("tensor", "s_gdma", sch.val("s_gdma"))
    W("tensor", "s_dev", sch.val("s_dev"))
    W("tensor", "s_msk", 1)
    for k in range(3):
        for i in range(4):
            E("tensor", lambda pe, k=k, i=i: pe.matmul(
                out=pY[4 + i][:],
                lhsT=ones_f32[0:1, :],
                rhs=gb_row[0:1, k * D + i * 512:k * D + (i + 1) * 512],
                start=True, stop=True))
        sch.tag("tensor", "s_gb")
        tgt = [gam_bc, bet_bc, bo_bc][k]
        W("vector", "s_gb", sch.val("s_gb"))
        for i in range(4):
            E("vector", lambda v, tgt=tgt, i=i: v.tensor_copy(
                out=tgt[:, i * 512:(i + 1) * 512], in_=pY[4 + i][:]))
        sch.tag("vector", "s_gb")
        W("tensor", "s_gb", sch.val("s_gb"))

    for t in range(NTO):
        yt = y_sb[:, t * D:(t + 1) * D]
        W("vector", "s_jA", sch.val("s_jA"))
        E("vector", lambda v, yt=yt, t=t: v.tensor_scalar(
            out=junk3[:], in0=yt, scalar1=1.0, scalar2=0.0, op0=ALU.mult, op1=ALU.add,
            accum_out=stc("sx", t)), inc="s_jA")
        W("vector", "s_jA", sch.val("s_jA"))
        E("vector", lambda v, yt=yt, t=t: v.scalar_tensor_tensor(
            out=junk3[:], in0=yt, scalar=1.0, in1=yt, op0=ALU.mult, op1=ALU.mult,
            accum_out=stc("sq", t)), inc="s_jA")
    E("vector", lambda v: v.drain())
    emit_ln_block(0, NTO, "s_lnst", "s_ost", "s_orstd", "s_lnr")
    for t in range(NTO):
        yt = y_sb[:, t * D:(t + 1) * D]
        W("vector", "s_lnr", sch.val("s_lnr"))
        if t >= 2:
            W("vector", "s_g", sch.val("s_g") - 1)
        E("vector", lambda v, yt=yt, t=t: v.tensor_scalar(
            out=yn_t[t % 2][:], in0=yt, scalar1=stc("negmu", t), scalar2=stc("rstd", t),
            op0=ALU.add, op1=ALU.mult), inc="s_jA")
        W("vector", "s_jA", sch.val("s_jA"))
        if t == 0:
            W("vector", "s_gb", sch.val("s_gb"))
        E("vector", lambda v, t=t: v.tensor_tensor(
            out=yn_t[t % 2][:], in0=yn_t[t % 2][:], in1=gam_bc[:], op=ALU.mult), inc="s_jA")
        W("vector", "s_jA", sch.val("s_jA"))
        E("vector", lambda v, t=t: v.tensor_tensor(
            out=yn_t[t % 2][:], in0=yn_t[t % 2][:], in1=bet_bc[:], op=ALU.add), inc="s_yn")
        W("scalar", "s_yn", sch.val("s_yn"))
        E("scalar", lambda s, t=t: s.activation(
            out=g_sb[:, t * D:(t + 1) * D], in_=yn_t[t % 2][:], func=AF.Silu), inc="s_g")

    if debug_outputs:
        W("gpsimd", "s_g", sch.val("s_g"))
        E("gpsimd", lambda s: s.dma_start(out=dbg["g"], in_=g_sb[:]), inc="s_dbg", inck=16)

    s_t = [mk_sbF(f"st{i}", [128, D], BF16) for i in range(2)]
    xo_t = [mk_sbF(f"xot{i}", [128, D], F32) for i in range(2)]
    o_t = [mk_sbF(f"ot{i}", [128, D], F32) for i in range(1)]

    W("sync", "s_coll", sch.val("s_coll"))
    for t in range(NTO):
        if t >= 2:
            W("sync", "s_fin", t - 1)
        E("sync", lambda s, t=t: s.dma_start(
            out=s_t[t % 2][:], in_=ps_recv[t * 128:(t + 1) * 128, :]), inc=f"s_sd{t % 2}", inck=16)
        E("sync", lambda s, t=t: s.dma_start(
            out=xo_t[t % 2][:], in_=xo_d[t * 128:(t + 1) * 128, :]), inc=f"s_xo{t % 2}", inck=16)
        W("vector", f"s_sd{t % 2}", 16 * (t // 2 + 1))
        W("vector", f"s_xo{t % 2}", 16 * (t // 2 + 1))
        W("vector", "s_g", sch.val("s_g"))
        if t >= 1:
            W("vector", "s_out", 16 * t)
        E("vector", lambda v, t=t: v.tensor_tensor(
            out=o_t[0][:], in0=s_t[t % 2][:], in1=bo_bc[:], op=ALU.add), inc="s_jA")
        W("vector", "s_jA", sch.val("s_jA"))
        E("vector", lambda v, t=t: v.tensor_tensor(
            out=o_t[0][:], in0=o_t[0][:], in1=g_sb[:, t * D:(t + 1) * D], op=ALU.mult), inc="s_jA")
        W("vector", "s_jA", sch.val("s_jA"))
        E("vector", lambda v, t=t: v.tensor_tensor(
            out=o_t[0][:], in0=o_t[0][:], in1=xo_t[t % 2][:], op=ALU.add), inc="s_fin")
        W("sync", "s_fin", sch.val("s_fin"))
        E("sync", lambda s, t=t: s.dma_start(
            out=out_d[t * 128:(t + 1) * 128, :], in_=o_t[0][:]), inc="s_out", inck=16)

    W("sync", "s_out", 16 * NTO)
    if debug_outputs:
        W("gpsimd", "s_dbg", sch.val("s_dbg"))

    with nc.Block() as block:
        @block.sync
        def _(eng):
            for f in sch.ops["sync"]:
                f(eng)

        @block.gpsimd
        def _(eng):
            for f in sch.ops["gpsimd"]:
                f(eng)

        @block.vector
        def _(eng):
            for f in sch.ops["vector"]:
                f(eng)

        @block.scalar
        def _(eng):
            for f in sch.ops["scalar"]:
                f(eng)

        @block.tensor
        def _(eng):
            for f in sch.ops["tensor"]:
                f(eng)

    close_all()
    return nc


def kernel(L0_attn_hidden_states, L0_attn_Wq, L0_attn_Wk, L0_attn_Wv, L0_attn_Wo,
           L0_attn_bo, L0_ffn_gate_w, L0_ffn_up_w, L0_ffn_down_w,
           L0_ffn_expert_weight, L0_ffn_ln_gamma, L0_ffn_ln_beta):
    S, D_ = 8192, 2048
    TOK_OWN = S // N_CORES
    x = np.ascontiguousarray(np.asarray(L0_attn_hidden_states, dtype=np.float32).reshape(S, D_))
    wq = np.asarray(L0_attn_Wq, dtype=np.float32)
    wk = np.asarray(L0_attn_Wk, dtype=np.float32)
    wv = np.asarray(L0_attn_Wv, dtype=np.float32)
    wo = np.asarray(L0_attn_Wo, dtype=np.float32)
    bo = np.ascontiguousarray(np.asarray(L0_attn_bo, dtype=np.float32).reshape(1, D_))
    gate = np.ascontiguousarray(np.asarray(L0_ffn_gate_w, dtype=np.float32))
    up = np.ascontiguousarray(np.asarray(L0_ffn_up_w, dtype=np.float32))
    down = np.ascontiguousarray(np.asarray(L0_ffn_down_w, dtype=np.float32))
    expw = np.ascontiguousarray(np.asarray(L0_ffn_expert_weight, dtype=np.float32))
    gamma = np.ascontiguousarray(np.asarray(L0_ffn_ln_gamma, dtype=np.float32).reshape(1, D_))
    beta = np.ascontiguousarray(np.asarray(L0_ffn_ln_beta, dtype=np.float32).reshape(1, D_))

    nc = build_program(S=S, TOK_OWN=TOK_OWN)
    in_maps = []
    for c in range(N_CORES):
        in_maps.append({
            "x": x,
            "x_own": np.ascontiguousarray(x[c * TOK_OWN:(c + 1) * TOK_OWN]),
            "wq": np.ascontiguousarray(wq[:, c * 256:(c + 1) * 256]),
            "wk": np.ascontiguousarray(wk[:, c * 256:(c + 1) * 256]),
            "wv": np.ascontiguousarray(wv[:, c * 256:(c + 1) * 256]),
            "wo": np.ascontiguousarray(wo[c * 256:(c + 1) * 256, :]),
            "bo": bo, "gate": gate, "up": up, "down": down, "expw": expw,
            "gamma": gamma, "beta": beta,
        })
    res = run_bass_kernel_spmd(nc, in_maps, core_ids=list(range(N_CORES)))
    out = np.concatenate([res.results[c]["out"] for c in range(N_CORES)], axis=0)
    return out.reshape(1, S, D_)


# version 10
# speedup vs baseline: 1.0228x; 1.0228x over previous
import numpy as np
import concourse.bass as bass
import concourse.mybir as mybir
from concourse.bass_utils import run_bass_kernel_spmd

F32 = mybir.dt.float32
BF16 = mybir.dt.bfloat16
AF = mybir.ActivationFunctionType
ALU = mybir.AluOpType
AX = mybir.AxisListType

N_CORES = 8
D = 2048
FC = D // 128
INTER = 8192
NEXP = 8
EPS = 1e-5
NH_LOC = 2
VST = 132

ENGINES = ("sync", "gpsimd", "vector", "scalar", "tensor")


class Sched:
    def __init__(self):
        self.ops = {e: [] for e in ENGINES}
        self.sems = {}
        self.counts = {}

    def wait(self, engine, sem, val):
        if val <= 0:
            return
        s = self.sems[sem]
        self.ops[engine].append(lambda eng, s=s, v=val: eng.wait_ge(s, v))

    def emit(self, engine, fn, inc=None, inck=1):
        if inc is None:
            self.ops[engine].append(fn)
            return None
        s = self.sems[inc]
        self.ops[engine].append(lambda eng, s=s, k=inck, f=fn: f(eng).then_inc(s, k))
        self.counts[inc] += inck
        return self.counts[inc]

    def val(self, sem):
        return self.counts[sem]

    def tag(self, engine, inc, inck=1):
        s = self.sems[inc]
        prev = self.ops[engine].pop()
        self.ops[engine].append(lambda eng, s=s, k=inck, f=prev: f(eng).then_inc(s, k))
        self.counts[inc] += inck
        return self.counts[inc]


def build_program(S=8192, TOK_OWN=1024, debug_outputs=False):
    NT = S // 128
    NB = S // 512
    NTO = TOK_OWN // 128
    NTB = TOK_OWN // 512
    NISL = INTER // 128

    nc = bass.Bass("TRN2", target_bir_lowering=False)

    x_d = nc.dram_tensor("x", [S, D], F32, kind="ExternalInput").ap()
    xo_d = nc.dram_tensor("x_own", [TOK_OWN, D], F32, kind="ExternalInput").ap()
    wq_d = nc.dram_tensor("wq", [D, 256], F32, kind="ExternalInput").ap()
    wk_d = nc.dram_tensor("wk", [D, 256], F32, kind="ExternalInput").ap()
    wv_d = nc.dram_tensor("wv", [D, 256], F32, kind="ExternalInput").ap()
    wo_d = nc.dram_tensor("wo", [256, D], F32, kind="ExternalInput").ap()
    bo_d = nc.dram_tensor("bo", [1, D], F32, kind="ExternalInput").ap()
    gate_d = nc.dram_tensor("gate", [D, NEXP], F32, kind="ExternalInput").ap()
    up_d = nc.dram_tensor("up", [D, INTER], F32, kind="ExternalInput").ap()
    down_d = nc.dram_tensor("down", [INTER, D], F32, kind="ExternalInput").ap()
    expw_d = nc.dram_tensor("expw", [D, D], F32, kind="ExternalInput").ap()
    gamma_d = nc.dram_tensor("gamma", [1, D], F32, kind="ExternalInput").ap()
    beta_d = nc.dram_tensor("beta", [1, D], F32, kind="ExternalInput").ap()
    out_d = nc.dram_tensor("out", [TOK_OWN, D], F32, kind="ExternalOutput").ap()
    ps_send = nc.dram_tensor("ps_send", [S, D], BF16).ap()
    ps_recv = nc.dram_tensor("ps_recv", [TOK_OWN, D], BF16).ap()
    upbf = nc.dram_tensor("upbf", [(INTER // 128) * 128, D], BF16).ap()
    downbf = nc.dram_tensor("downbf", [4 * (INTER // 512) * 128, D], BF16).ap()
    expbf = nc.dram_tensor("expbf", [4 * 128, FC * 512], BF16).ap()
    gatebf = nc.dram_tensor("gatebf", [128, FC * NEXP], BF16).ap()

    dbg = {}
    if debug_outputs:
        dbg["qT"] = nc.dram_tensor("dbg_qT", [128, NH_LOC * S], F32, kind="ExternalOutput").ap()
        dbg["kT"] = nc.dram_tensor("dbg_kT", [128, NH_LOC * S], F32, kind="ExternalOutput").ap()
        dbg["v"] = nc.dram_tensor("dbg_v", [128, NH_LOC * NT * VST], F32, kind="ExternalOutput").ap()
        dbg["ctx"] = nc.dram_tensor("dbg_ctx", [128, NH_LOC * S], F32, kind="ExternalOutput").ap()
        dbg["y"] = nc.dram_tensor("dbg_y", [128, NTO * D], F32, kind="ExternalOutput").ap()
        dbg["g"] = nc.dram_tensor("dbg_g", [128, NTO * D], F32, kind="ExternalOutput").ap()
        dbg["xnT"] = nc.dram_tensor("dbg_xnT", [128, FC * TOK_OWN], F32, kind="ExternalOutput").ap()

    sch = Sched()
    sem_names = [
        "s_x", "s_stats", "s_rstd", "s_xn", "s_ptr", "s_xnT",
        "s_pqk", "s_qkev", "s_pv", "s_vev", "s_wst", "s_msk",
        "s_S", "s_P", "s_Pm", "s_AV", "s_ctx",
        "s_ctxT", "s_ctxTe", "s_wops", "s_psst", "s_psdma", "s_coll",
        "s_gdma", "s_glog", "s_scl",
        "s_ownx", "s_ost", "s_orstd", "s_oxn", "s_optr", "s_oxnT",
        "s_exdma", "s_ey", "s_yev",
        "s_updma", "s_up", "s_hrel", "s_h2",
        "s_dwdma", "s_dwn", "s_dev",
        "s_gb", "s_lnst", "s_lnr", "s_yn", "s_g",
        "s_sdma", "s_xodma", "s_fin", "s_out", "s_dbg", "s_pcast", "s_jA", "s_jB",
        "s_x0", "s_x1", "s_x2", "s_ox0", "s_ox1", "s_ox2",
        "s_upd0", "s_upd1", "s_dwd0", "s_dwd1", "s_exd0", "s_exd1",
        "s_sd0", "s_sd1", "s_xo0", "s_xo1", "s_ps0", "s_ps1",
    ]

    open_cms = []

    def close_all():
        for cm in reversed(open_cms):
            cm.__exit__(None, None, None)

    def mk(cm):
        t = cm.__enter__()
        open_cms.append(cm)
        return t, cm

    def mk_sb(name, shape, dtype):
        return mk(nc.sbuf_tensor(name, shape, dtype))

    def mk_ps(name, shape, dtype):
        return mk(nc.psum_tensor(name, shape, dtype))

    def free(cms):
        for cm in reversed(cms):
            cm.__exit__(None, None, None)
            open_cms.remove(cm)

    for s in sem_names:
        h, _ = mk(nc.semaphore(s))
        sch.sems[s] = h
        sch.counts[s] = 0

    E = sch.emit
    W = sch.wait

    ident, _ = mk_sb("ident", [128, 128], BF16)
    trimask, _ = mk_sb("trimask", [128, 128], BF16)
    ones_f32, _ = mk_sb("ones_f32", [1, 128], F32)
    NTT = NT + NTO
    stats, _ = mk_sb("stats", [128, 8 * NTT], F32)
    xnTo, _ = mk_sb("xnTo", [128, FC * TOK_OWN], BF16)

    def stc(kind, t, w=1):
        i = {"sx": 0, "sq": 1, "negmu": 2, "m2e": 3, "musq": 4,
             "vare": 5, "lnv": 6, "rstd": 7}[kind]
        return stats[:, i * NTT + t:i * NTT + t + w]

    def p0(g):
        g.memset(ident[:], 1.0)
        g.drain()
        g.affine_select(out=ident[:], in_=ident[:], pattern=[[-1, 128]],
                        compare_op=ALU.is_equal, fill=0.0, base=0, channel_multiplier=1)
        g.memset(trimask[:], 1.0)
        g.drain()
        g.affine_select(out=trimask[:], in_=trimask[:], pattern=[[1, 128]],
                        compare_op=ALU.is_ge, fill=0.0, base=0, channel_multiplier=-1)
        return g.memset(ones_f32[:], 1.0)

    E("gpsimd", p0, inc="s_msk")

    def emit_ln_accum(xb, t, jk, wait_sem, wait_val):
        jk1, jk2 = jk
        W("vector", wait_sem, wait_val)
        W("vector", "s_jA", sch.val("s_jA"))
        E("vector", lambda v, xb=xb, t=t, jk1=jk1: v.tensor_scalar(
            out=jk1[:], in0=xb[:], scalar1=1.0, scalar2=0.0, op0=ALU.mult, op1=ALU.add,
            accum_out=stc("sx", t)), inc="s_jA")
        W("vector", "s_jB", sch.val("s_jB"))
        E("vector", lambda v, xb=xb, t=t, jk2=jk2: v.scalar_tensor_tensor(
            out=jk2[:], in0=xb[:], scalar=1.0, in1=xb[:], op0=ALU.mult, op1=ALU.mult,
            accum_out=stc("sq", t)), inc="s_jB")

    def emit_ln_block(t0, w, sem_stats, sem_musq, sem_vare, sem_rstd):
        E("vector", lambda v: v.drain())
        E("vector", lambda v, t0=t0, w=w: v.tensor_scalar(
            out=stc("negmu", t0, w), in0=stc("sx", t0, w), scalar1=-1.0 / D,
            scalar2=None, op0=ALU.mult))
        E("vector", lambda v, t0=t0, w=w: v.tensor_scalar(
            out=stc("m2e", t0, w), in0=stc("sq", t0, w), scalar1=1.0 / D, scalar2=EPS,
            op0=ALU.mult, op1=ALU.add), inc=sem_stats)
        W("scalar", sem_stats, sch.val(sem_stats))
        E("scalar", lambda s, t0=t0, w=w: s.activation(
            out=stc("musq", t0, w), in_=stc("negmu", t0, w), func=AF.Square), inc=sem_musq)
        W("vector", sem_musq, sch.val(sem_musq))
        E("vector", lambda v, t0=t0, w=w: v.tensor_tensor(
            out=stc("vare", t0, w), in0=stc("m2e", t0, w), in1=stc("musq", t0, w),
            op=ALU.subtract), inc=sem_vare)
        W("scalar", sem_vare, sch.val(sem_vare))
        E("scalar", lambda s, t0=t0, w=w: s.activation(
            out=stc("lnv", t0, w), in_=stc("vare", t0, w), func=AF.Ln))
        E("scalar", lambda s: s.drain())
        E("scalar", lambda s, t0=t0, w=w: s.activation(
            out=stc("rstd", t0, w), in_=stc("lnv", t0, w), func=AF.Exp, scale=-0.5),
          inc=sem_rstd)

    def emit_ln_norm(xb, t, sem_rstd, sem_xn):
        W("vector", sem_rstd, sch.val(sem_rstd))
        E("vector", lambda v, xb=xb, t=t: v.tensor_scalar(
            out=xb[:], in0=xb[:], scalar1=stc("negmu", t), scalar2=stc("rstd", t),
            op0=ALU.add, op1=ALU.mult), inc=sem_xn)

    def emit_transpose16(xb, t, banks, sem_ptr, sem_evac, dst_fn, wait_sem):
        W("tensor", wait_sem, sch.val(wait_sem))
        for half in range(2):
            bank = banks[half]
            if t >= 1:
                W("tensor", sem_evac, (t - 1) * 2 + half + 1)
            for j in range(8):
                c = half * 8 + j
                E("tensor", lambda pe, xb=xb, bank=bank, j=j, c=c: pe.transpose(
                    out=bank[:, j * 128:(j + 1) * 128], in_=xb[:, c * 128:(c + 1) * 128],
                    identity=ident[:]))
            sch.tag("tensor", sem_ptr)
        for half in range(2):
            W("scalar", sem_ptr, sch.val(sem_ptr) - (1 - half))
            dst = dst_fn(half)
            src = banks[half][:].rearrange("p (c q) -> p c q", c=8)
            E("scalar", lambda s, dst=dst, src=src: s.copy(out=dst, in_=src), inc=sem_evac)

    b_cms = []

    def mk_sbB(name, shape, dtype):
        t, cm = mk_sb(name, shape, dtype)
        b_cms.append(cm)
        return t

    qT = mk_sbB("qT", [128, NH_LOC * S], BF16)
    kT = mk_sbB("kT", [128, NH_LOC * S], BF16)
    vb = mk_sbB("vb", [128, NH_LOC * NT * VST], BF16)

    b1_cms = []

    def mk_sbB1(name, shape, dtype):
        t, cm = mk_sb(name, shape, dtype)
        b1_cms.append(cm)
        return t

    wqk_st = mk_sbB1("wqk_st", [128, FC * 512], BF16)
    wv_st = mk_sbB1("wv_st", [128, FC * 256], BF16)
    xbf = [mk_sbB1(f"xbf{i}", [128, D], BF16) for i in range(6)]
    junk = (mk_sbB1("junkbA", [128, D], BF16), mk_sbB1("junkbB", [128, D], BF16))
    xnT = [mk_sbB1("xnT0", [128, FC * 512], BF16)]

    ps1_cms = []

    def mk_psS1(name, shape, dtype):
        t, cm = mk_ps(name, shape, dtype)
        ps1_cms.append(cm)
        return t

    ptrb = [mk_psS1(f"ptrb{i}", [128, 1024], BF16) for i in range(2)]
    pqk = [mk_psS1(f"pqk{i}", [128, 512], F32) for i in range(4)]
    pv = [mk_psS1(f"pv{i}", [128, 256], F32) for i in range(2)]

    E("vector", lambda v: v.memset(vb[:], 1.0))
    E("vector", lambda v: v.drain())

    def wdma(fn):
        E("gpsimd", fn, inc="s_wst", inck=16)

    wqk_v = wqk_st[:].rearrange("p (c z) -> p c z", c=FC)
    for part in range(4):
        lo, hi = part * 4, part * 4 + 4
        wdma(lambda g, lo=lo, hi=hi: g.dma_start(
            out=wqk_v[:, lo:hi, 0:256],
            in_=wq_d.rearrange("(c p) q -> p c q", p=128)[:, lo:hi, :]))
        wdma(lambda g, lo=lo, hi=hi: g.dma_start(
            out=wqk_v[:, lo:hi, 256:512],
            in_=wk_d.rearrange("(c p) q -> p c q", p=128)[:, lo:hi, :]))
        wdma(lambda g, lo=lo, hi=hi: g.dma_start(
            out=wv_st[:].rearrange("p (c q) -> p c q", c=FC)[:, lo:hi, :],
            in_=wv_d.rearrange("(c p) q -> p c q", p=128)[:, lo:hi, :]))
    WST_ALL = sch.val("s_wst")

    XSLOT = ["s_x0", "s_x1", "s_x2", "s_ox0", "s_ox1", "s_ox2"]
    for t in range(NT + NTO):
        b, ti = t // 4, t % 4
        xb = xbf[t % 6]
        if t >= 6:
            W("gpsimd", "s_ptr", 2 * (t - 5))
        if t < NT:
            E("gpsimd", lambda g, xb=xb, t=t: g.dma_start(
                out=xb[:], in_=x_d[t * 128:(t + 1) * 128, :]), inc=XSLOT[t % 6], inck=16)
        else:
            to = t - NT
            E("gpsimd", lambda g, xb=xb, to=to: g.dma_start(
                out=xb[:], in_=xo_d[to * 128:(to + 1) * 128, :]), inc=XSLOT[t % 6], inck=16)
        emit_ln_accum(xb, t, junk, XSLOT[t % 6], 16 * (t // 6 + 1))
        if t == 0:
            W("tensor", "s_msk", 1)
            W("tensor", "s_wst", WST_ALL)
        if ti != 3:
            continue
        emit_ln_block(t - 3, 4, "s_stats", "s_ost", "s_orstd", "s_rstd")
        for tt in range(4):
            emit_ln_norm(xbf[(t - 3 + tt) % 6], t - 3 + tt, "s_rstd", "s_xn")
        xt = xnT[0]
        for tt in range(4):
            tg = t - 3 + tt
            if tg < NT:
                dst_fn = (lambda half, xt=xt, tt=tt: xt[:].rearrange("p (c q) -> p c q", c=FC)[
                    :, half * 8:(half + 1) * 8, tt * 128:(tt + 1) * 128])
            else:
                to = tg - NT
                dst_fn = (lambda half, to=to: xnTo[:].rearrange("p (c q) -> p c q", c=FC)[
                    :, half * 8:(half + 1) * 8, to * 128:(to + 1) * 128])
            emit_transpose16(xbf[tg % 6], tg, ptrb, "s_ptr", "s_xnT", dst_fn, "s_xn")

        if t < NT:
            W("tensor", "s_xnT", sch.val("s_xnT"))
            if b >= 1:
                W("tensor", "s_qkev", b)
            for sl in range(4):
                for c in range(FC):
                    E("tensor", lambda pe, sl=sl, c=c, xt=xt: pe.matmul(
                        out=pqk[sl][:],
                        lhsT=wqk_st[:, c * 512 + sl * 128:c * 512 + (sl + 1) * 128],
                        rhs=xt[:, c * 512:(c + 1) * 512],
                        start=(c == 0), stop=(c == FC - 1)))
            sch.tag("tensor", "s_pqk")
            for tt in range(4):
                kt_g = b * 4 + tt
                if kt_g >= 2:
                    W("tensor", "s_vev", kt_g - 1)
                for c in range(FC):
                    E("tensor", lambda pe, tt=tt, c=c, xt=xt: pe.matmul(
                        out=pv[tt % 2][:],
                        lhsT=xt[:, c * 512 + tt * 128:c * 512 + (tt + 1) * 128],
                        rhs=wv_st[:, c * 256:(c + 1) * 256],
                        start=(c == 0), stop=(c == FC - 1)))
                sch.tag("tensor", "s_pv")
            W("vector", "s_pqk", sch.val("s_pqk"))
            for sl in range(4):
                tgt = qT if sl < 2 else kT
                h = sl % 2
                E("vector", lambda v, sl=sl, tgt=tgt, h=h, b=b: v.tensor_copy(
                    out=tgt[:, h * S + b * 512:h * S + (b + 1) * 512], in_=pqk[sl][:]))
            sch.tag("vector", "s_qkev")
            for tt in range(4):
                kt_g = b * 4 + tt
                W("vector", "s_pv", sch.val("s_pv") - (3 - tt))
                for h in range(NH_LOC):
                    E("vector", lambda v, tt=tt, h=h, kt_g=kt_g: v.tensor_copy(
                        out=vb[:, h * NT * VST + kt_g * VST:h * NT * VST + kt_g * VST + 128],
                        in_=pv[tt % 2][:, h * 128:(h + 1) * 128]))
                sch.tag("vector", "s_vev")

    P1_XNT_ALL = sch.val("s_xnT")

    if debug_outputs:
        W("gpsimd", "s_qkev", sch.val("s_qkev"))
        W("gpsimd", "s_vev", sch.val("s_vev"))
        E("gpsimd", lambda s: s.dma_start(out=dbg["qT"], in_=qT[:]), inc="s_dbg", inck=16)
        E("gpsimd", lambda s: s.dma_start(out=dbg["kT"], in_=kT[:]), inc="s_dbg", inck=16)
        E("gpsimd", lambda s: s.dma_start(out=dbg["v"], in_=vb[:]), inc="s_dbg", inck=16)

    free(ps1_cms[:])
    free(b1_cms[:])

    b2_cms = []

    def mk_sbB2(name, shape, dtype):
        t, cm = mk_sb(name, shape, dtype)
        b2_cms.append(cm)
        return t

    ctxb = mk_sbB2("ctxb", [128, NH_LOC * S], BF16)
    wo_st = mk_sbB2("wo_st", [128, 2 * D], BF16)
    W("gpsimd", "s_vev", sch.val("s_vev"))
    W("gpsimd", "s_pv", sch.val("s_pv"))
    W("gpsimd", "s_xnT", sch.val("s_xnT"))
    E("gpsimd", lambda g: g.dma_start(
        out=wo_st[:].rearrange("p (c q) -> p c q", c=2),
        in_=wo_d.rearrange("(c p) q -> p c q", p=128)), inc="s_wst", inck=16)
    upbf_v = upbf.rearrange("(s p) x -> s p x", p=128)
    for isl in range(INTER // 128):
        E("gpsimd", lambda g, isl=isl: g.dma_start(
            out=upbf_v[isl].rearrange("p (c i) -> p c i", i=128),
            in_=up_d.rearrange("(c p) (s i) -> s p c i", p=128, i=128)[isl]),
          inc="s_pcast", inck=16)
    dnbf_v = downbf.rearrange("(f g p) x -> f g p x", g=INTER // 512, p=128)
    for fb in range(4):
        for grp in range(INTER // 512):
            E("gpsimd", lambda g, fb=fb, grp=grp: g.dma_start(
                out=dnbf_v[fb, grp].rearrange("p (c q) -> p c q", c=4),
                in_=down_d.rearrange("(g c p) q -> g p c q", p=128, c=4)[grp][:, :,
                                                                             fb * 512:(fb + 1) * 512]),
              inc="s_pcast", inck=16)
    exbf_v = expbf.rearrange("(f p) x -> f p x", p=128)
    for fb in range(4):
        for half in range(2):
            E("gpsimd", lambda g, fb=fb, half=half: g.dma_start(
                out=exbf_v[fb].rearrange("p (c q) -> p c q", c=16)[:, half * 8:(half + 1) * 8, :],
                in_=expw_d.rearrange("(c p) q -> p c q", p=128)[
                    :, half * 8:(half + 1) * 8, fb * 512:(fb + 1) * 512]),
              inc="s_pcast", inck=16)
    E("gpsimd", lambda g: g.dma_start(
        out=gatebf.rearrange("p (c e) -> p c e", c=FC),
        in_=gate_d.rearrange("(c p) e -> p c e", p=128)), inc="s_pcast", inck=16)
    PCAST_ALL = sch.val("s_pcast")
    pTb = [mk_sbB2(f"pT{i}", [128, 1536], BF16) for i in range(2)]
    pdiag = mk_sbB2("pdiag", [128, 4 * 128], BF16)
    rc4 = mk_sbB2("rc4", [128, 4], F32)
    ctxT_sb = [mk_sbB2(f"ctxT{i}", [128, 256], BF16) for i in range(2)]
    ps_stage = [mk_sbB2(f"psst{i}", [128, D], BF16) for i in range(2)]

    ps2_cms = []
    pS = []
    for i in range(2):
        t_, cm = mk_ps(f"pS{i}", [128, 1536], F32)
        ps2_cms.append(cm)
        pS.append(t_)
    pC = []
    for i in range(2):
        t_, cm = mk_ps(f"pC{i}", [128, 512], F32)
        ps2_cms.append(cm)
        pC.append(t_)

    def ctx_ps(qt):
        return pC[qt // 2][:, (qt % 2) * 256:(qt % 2) * 256 + 129]

    W("tensor", "s_qkev", sch.val("s_qkev"))
    W("tensor", "s_vev", sch.val("s_vev"))
    W("tensor", "s_xnT", P1_XNT_ALL)

    for h in range(NH_LOC):
        hq, hv = h * S, h * NT * VST
        for qb in range(NB):
            nk = 4 * (qb + 1)
            ngroups = (nk + 2) // 3
            p_cnt = {}
            av_cnt = {}
            pm_cnt = {}

            def emit_S(g):
                j0, j1 = 3 * g, min(3 * g + 3, nk)
                bank = pS[g % 2]
                if g >= 2:
                    W("tensor", "s_P", p_cnt[g - 2])
                if g == 0:
                    W("tensor", "s_ctx", sch.val("s_ctx"))
                for j in range(j0, j1):
                    E("tensor", lambda pe, j=j, j0=j0, bank=bank, hq=hq, qb=qb: pe.matmul(
                        out=bank[:, (j - j0) * 512:(j - j0 + 1) * 512],
                        lhsT=kT[:, hq + j * 128:hq + (j + 1) * 128],
                        rhs=qT[:, hq + qb * 512:hq + (qb + 1) * 512],
                        start=True, stop=True))
                sch.tag("tensor", "s_S")

            def emit_exp(g):
                j0, j1 = 3 * g, min(3 * g + 3, nk)
                n = (j1 - j0) * 512
                W("scalar", "s_S", sch.val("s_S"))
                if g >= 2:
                    W("scalar", "s_AV", av_cnt[g - 2])
                E("scalar", lambda s, g=g, n=n: s.activation(
                    out=pTb[g % 2][:, 0:n], in_=pS[g % 2][:, 0:n], func=AF.Exp), inc="s_P")
                p_cnt[g] = sch.val("s_P")
                dt_ = [(j, j - 4 * qb) for j in range(j0, j1) if j >= 4 * qb]
                for (j, qt) in dt_:
                    W("vector", "s_P", p_cnt[g])
                    E("vector", lambda v, g=g, j=j, j0=j0, qt=qt: v.tensor_tensor(
                        out=pdiag[:, qt * 128:(qt + 1) * 128],
                        in0=pTb[g % 2][:, (j - j0) * 512 + qt * 128:(j - j0) * 512 + (qt + 1) * 128],
                        in1=trimask[:], op=ALU.mult), inc="s_Pm")
                pm_cnt[g] = sch.val("s_Pm")

            def emit_AV(g):
                j0, j1 = 3 * g, min(3 * g + 3, nk)
                W("tensor", "s_P", p_cnt[g])
                if pm_cnt.get(g, 0):
                    W("tensor", "s_Pm", pm_cnt[g])
                for j in range(j0, j1):
                    for qt in range(4):
                        jmax = 4 * qb + qt
                        if j > jmax:
                            continue
                        if j == jmax:
                            lhs = pdiag[:, qt * 128:(qt + 1) * 128]
                        else:
                            lhs = pTb[g % 2][:, (j - j0) * 512 + qt * 128:(j - j0) * 512 + (qt + 1) * 128]
                        E("tensor", lambda pe, lhs=lhs, qt=qt, j=j, jmax=jmax, hv=hv: pe.matmul(
                            out=ctx_ps(qt), lhsT=lhs,
                            rhs=vb[:, hv + j * VST:hv + j * VST + 129],
                            start=(j == 0 and qt % 2 == 0),
                            stop=(j == jmax and qt % 2 == 1)))
                sch.tag("tensor", "s_AV")
                av_cnt[g] = sch.val("s_AV")

            for g in range(ngroups):
                emit_S(g)
                emit_exp(g)
                if g >= 1:
                    emit_AV(g - 1)
            emit_AV(ngroups - 1)

            W("vector", "s_AV", av_cnt[ngroups - 1])
            for qt in range(4):
                E("vector", lambda v, qt=qt: v.reciprocal(
                    out=rc4[:, qt:qt + 1], in_=ctx_ps(qt)[:, 128:129]))
            E("vector", lambda v: v.drain())
            for qt in range(4):
                tix = qb * 4 + qt
                E("vector", lambda v, qt=qt, tix=tix, hq=hq: v.tensor_scalar(
                    out=ctxb[:, hq + tix * 128:hq + (tix + 1) * 128],
                    in0=ctx_ps(qt)[:, 0:128],
                    scalar1=rc4[:, qt:qt + 1], scalar2=None, op0=ALU.mult))
            sch.tag("vector", "s_ctx")

    if debug_outputs:
        W("gpsimd", "s_ctx", sch.val("s_ctx"))
        E("gpsimd", lambda s: s.dma_start(out=dbg["ctx"], in_=ctxb[:]), inc="s_dbg", inck=16)

    free(ps2_cms[:])

    ps25_cms = []
    pCt = []
    for i in range(2):
        t_, cm = mk_ps(f"pCt{i}", [128, 256], BF16)
        ps25_cms.append(cm)
        pCt.append(t_)
    pWo = []
    for i in range(4):
        t_, cm = mk_ps(f"pWo{i}", [128, 512], F32)
        ps25_cms.append(cm)
        pWo.append(t_)

    W("tensor", "s_wst", sch.val("s_wst"))
    for t in range(NT):
        if t >= 2:
            W("tensor", "s_ctxTe", t - 1)
        for h in range(NH_LOC):
            E("tensor", lambda pe, t=t, h=h: pe.transpose(
                out=pCt[t % 2][:, h * 128:(h + 1) * 128],
                in_=ctxb[:, h * S + t * 128:h * S + (t + 1) * 128],
                identity=ident[:]))
        sch.tag("tensor", "s_ctxT")
        W("vector", "s_ctxT", sch.val("s_ctxT"))
        E("vector", lambda v, t=t: v.tensor_copy(
            out=ctxT_sb[t % 2][:], in_=pCt[t % 2][:]), inc="s_ctxTe")
        W("tensor", "s_ctxTe", sch.val("s_ctxTe"))
        for fb in range(4):
            if t >= 1:
                W("tensor", "s_psst", (t - 1) * 4 + fb + 1)
            for ch in range(2):
                E("tensor", lambda pe, t=t, fb=fb, ch=ch: pe.matmul(
                    out=pWo[fb][:],
                    lhsT=ctxT_sb[t % 2][:, ch * 128:(ch + 1) * 128],
                    rhs=wo_st[:, ch * D + fb * 512:ch * D + (fb + 1) * 512],
                    start=(ch == 0), stop=(ch == 1)))
            sch.tag("tensor", "s_wops")
            W("scalar", "s_wops", sch.val("s_wops"))
            if fb == 0 and t >= 2:
                W("scalar", f"s_ps{t % 2}", 16 * (t // 2))
            E("scalar", lambda s, t=t, fb=fb: s.copy(
                out=ps_stage[t % 2][:, fb * 512:(fb + 1) * 512], in_=pWo[fb][:]), inc="s_psst")
        W("sync", "s_psst", sch.val("s_psst"))
        E("sync", lambda s, t=t: s.dma_start(
            out=ps_send[t * 128:(t + 1) * 128, :], in_=ps_stage[t % 2][:]), inc=f"s_ps{t % 2}", inck=16)

    W("gpsimd", "s_ps0", 16 * (NT // 2))
    W("gpsimd", "s_ps1", 16 * (NT // 2))
    E("gpsimd", lambda g: g.collective_compute(
        "ReduceScatter", ALU.add, replica_groups=[list(range(N_CORES))],
        ins=[ps_send.opt()], outs=[ps_recv.opt()]), inc="s_coll")

    free(ps25_cms[:])
    free(b2_cms[:])
    free(b_cms[:])

    f_cms = []

    def mk_sbF(name, shape, dtype):
        t, cm = mk_sb(name, shape, dtype)
        f_cms.append(cm)
        return t

    gate_st = mk_sbF("gate_st", [128, FC * NEXP], BF16)
    y_sb = mk_sbF("y_sb", [128, NTO * D], BF16)
    logits = mk_sbF("logits", [128, NTO * NEXP], F32)
    l_tmp = mk_sbF("l_tmp", [128, NEXP], F32)
    max12 = mk_sbF("max12", [128, 2], F32)
    rscl = mk_sbF("rscl", [128, NTO], F32)

    W("sync", "s_wops", sch.val("s_wops"))
    W("sync", "s_ps0", 16 * (NT // 2))
    W("sync", "s_ps1", 16 * (NT // 2))
    W("sync", "s_pcast", PCAST_ALL)
    if debug_outputs:
        W("sync", "s_dbg", sch.val("s_dbg"))
    E("sync", lambda s: s.dma_start(out=gate_st[:], in_=gatebf), inc="s_gdma", inck=16)

    if debug_outputs:
        W("gpsimd", "s_xnT", P1_XNT_ALL)
        E("gpsimd", lambda s: s.dma_start(out=dbg["xnT"], in_=xnTo[:]), inc="s_dbg", inck=16)

    psF_cms = []
    pY = []
    for i in range(8):
        t_, cm = mk_ps(f"pY{i}", [128, 512], F32)
        psF_cms.append(cm)
        pY.append(t_)

    W("tensor", "s_xnT", P1_XNT_ALL)
    W("tensor", "s_gdma", sch.val("s_gdma"))
    for t in range(NTO):
        if t >= 1:
            W("tensor", "s_glog", t)
        for c in range(FC):
            E("tensor", lambda pe, t=t, c=c: pe.matmul(
                out=pY[0][:, 0:NEXP],
                lhsT=xnTo[:, c * TOK_OWN + t * 128:c * TOK_OWN + (t + 1) * 128],
                rhs=gate_st[:, c * NEXP:(c + 1) * NEXP],
                start=(c == 0), stop=(c == FC - 1)))
        sch.tag("tensor", "s_S")
        W("vector", "s_S", sch.val("s_S"))
        E("vector", lambda v, t=t: v.tensor_copy(
            out=logits[:, t * NEXP:(t + 1) * NEXP], in_=pY[0][:, 0:NEXP]), inc="s_glog")

    for t in range(NTO):
        lg = logits[:, t * NEXP:(t + 1) * NEXP]
        E("vector", lambda v, lg=lg: v.tensor_reduce(
            out=max12[:, 0:1], in_=lg, axis=AX.X, op=ALU.max))
        E("vector", lambda v: v.drain())
        E("vector", lambda v, lg=lg: v.tensor_scalar(
            out=l_tmp[:], in0=lg, scalar1=max12[:, 0:1], scalar2=None, op0=ALU.is_equal))
        E("vector", lambda v: v.drain())
        E("vector", lambda v, lg=lg: v.scalar_tensor_tensor(
            out=l_tmp[:], in0=l_tmp[:], scalar=-3.0e38, in1=lg, op0=ALU.mult, op1=ALU.add))
        E("vector", lambda v: v.drain())
        E("vector", lambda v: v.tensor_reduce(
            out=max12[:, 1:2], in_=l_tmp[:], axis=AX.X, op=ALU.max))
        E("vector", lambda v: v.drain())
        E("vector", lambda v, t=t: v.tensor_tensor(
            out=rscl[:, t:t + 1], in0=max12[:, 0:1], in1=max12[:, 1:2], op=ALU.add))
        E("vector", lambda v: v.drain())
    sch.tag("vector", "s_scl")

    ex_cms = []

    def mk_sbEX(name, shape, dtype):
        t, cm = mk_sb(name, shape, dtype)
        ex_cms.append(cm)
        return t

    exslice = [mk_sbEX(f"exsl{i}", [128, FC * 512], BF16) for i in range(2)]
    for fb in range(4):
        if fb >= 2:
            W("sync", "s_ey", fb - 1)
        E("sync", lambda s, fb=fb: s.dma_start(
            out=exslice[fb % 2][:], in_=expbf[fb * 128:(fb + 1) * 128, :]),
          inc=f"s_exd{fb % 2}", inck=16)
        W("tensor", f"s_exd{fb % 2}", 16 * (fb // 2 + 1))
        if fb == 0:
            W("tensor", "s_glog", NTO)
        if fb >= 1:
            W("tensor", "s_yev", fb)
        for c in range(FC):
            for T in range(NTO):
                E("tensor", lambda pe, c=c, T=T, fb=fb: pe.matmul(
                    out=pY[T][:],
                    lhsT=xnTo[:, c * TOK_OWN + T * 128:c * TOK_OWN + (T + 1) * 128],
                    rhs=exslice[fb % 2][:, c * 512:(c + 1) * 512],
                    start=(c == 0), stop=(c == FC - 1)))
        sch.tag("tensor", "s_ey")
        W("vector", "s_ey", sch.val("s_ey"))
        if fb == 0:
            W("vector", "s_scl", 1)
        for T in range(NTO):
            E("vector", lambda v, T=T, fb=fb: v.tensor_scalar(
                out=y_sb[:, T * D + fb * 512:T * D + (fb + 1) * 512],
                in0=pY[T][:], scalar1=rscl[:, T:T + 1], scalar2=None, op0=ALU.mult))
        sch.tag("vector", "s_yev")

    free(ex_cms[:])
    ffnw_cms = []

    def mk_sbFW(name, shape, dtype):
        t, cm = mk_sb(name, shape, dtype)
        ffnw_cms.append(cm)
        return t

    h2T = mk_sbFW("h2T", [128, NISL * TOK_OWN], BF16)
    up_cms = []

    def mk_sbUP(name, shape, dtype):
        t, cm = mk_sb(name, shape, dtype)
        up_cms.append(cm)
        return t

    upst = [mk_sbUP(f"upst{i}", [128, D], BF16) for i in range(2)]
    hrel = [mk_sbUP(f"hrel{i}", [128, TOK_OWN], BF16) for i in range(2)]
    pH = [pY[0], pY[1]]

    for isl in range(NISL):
        ub = upst[isl % 2]
        if isl == 0:
            W("sync", "s_ey", 4)
        if isl >= 2:
            W("sync", "s_up", isl - 1)
        E("sync", lambda s, isl=isl, ub=ub: s.dma_start(
            out=ub[:], in_=upbf[isl * 128:(isl + 1) * 128, :]),
          inc=f"s_upd{isl % 2}", inck=16)
        W("tensor", f"s_upd{isl % 2}", 16 * (isl // 2 + 1))
        if isl == 0:
            W("tensor", "s_yev", 4)
        if isl >= 1:
            W("tensor", "s_hrel", isl)
        for c in range(FC):
            for tb in range(NTB):
                E("tensor", lambda pe, c=c, tb=tb, ub=ub: pe.matmul(
                    out=pH[tb][:],
                    lhsT=ub[:, c * 128:(c + 1) * 128],
                    rhs=xnTo[:, c * TOK_OWN + tb * 512:c * TOK_OWN + (tb + 1) * 512],
                    start=(c == 0), stop=(c == FC - 1)))
        sch.tag("tensor", "s_up")
        W("scalar", "s_up", sch.val("s_up"))
        if isl >= 2:
            W("scalar", "s_h2", isl - 1)
        for tb in range(NTB):
            E("scalar", lambda s, tb=tb, isl=isl: s.activation(
                out=hrel[isl % 2][:, tb * 512:(tb + 1) * 512], in_=pH[tb][:], func=AF.Relu))
        sch.tag("scalar", "s_hrel")
        W("vector", "s_hrel", sch.val("s_hrel"))
        E("vector", lambda v, isl=isl: v.tensor_tensor(
            out=h2T[:, isl * TOK_OWN:(isl + 1) * TOK_OWN],
            in0=hrel[isl % 2][:], in1=hrel[isl % 2][:], op=ALU.mult), inc="s_h2")

    free(up_cms[:])
    dwst = [mk_sbFW(f"dwst{i}", [128, 4 * 512], BF16) for i in range(2)]
    NGRP = NISL // 4
    W("tensor", "s_h2", sch.val("s_h2"))
    for fb in range(4):
        for grp in range(NGRP):
            gidx = fb * NGRP + grp
            if gidx < 2:
                W("sync", "s_up", NISL)
            if gidx >= 2:
                W("sync", "s_dwn", gidx - 1)
            E("sync", lambda s, fb=fb, grp=grp, gidx=gidx: s.dma_start(
                out=dwst[gidx % 2][:],
                in_=downbf[(fb * NGRP + grp) * 128:(fb * NGRP + grp + 1) * 128, :]),
              inc=f"s_dwd{gidx % 2}", inck=16)
            W("tensor", f"s_dwd{gidx % 2}", 16 * (gidx // 2 + 1))
            if grp == 0 and fb >= 1:
                W("tensor", "s_dev", fb)
            for i8 in range(4):
                isl = grp * 4 + i8
                for T in range(NTO):
                    E("tensor", lambda pe, gidx=gidx, i8=i8, isl=isl, T=T: pe.matmul(
                        out=pY[T][:],
                        lhsT=h2T[:, isl * TOK_OWN + T * 128:isl * TOK_OWN + (T + 1) * 128],
                        rhs=dwst[gidx % 2][:, i8 * 512:(i8 + 1) * 512],
                        start=(isl == 0), stop=(isl == NISL - 1)))
            sch.tag("tensor", "s_dwn")
        W("vector", "s_dwn", sch.val("s_dwn"))
        for T in range(NTO):
            yap = y_sb[:, T * D + fb * 512:T * D + (fb + 1) * 512]
            E("vector", lambda v, T=T, yap=yap: v.tensor_tensor(
                out=yap, in0=pY[T][:], in1=yap, op=ALU.add))
        sch.tag("vector", "s_dev")

    if debug_outputs:
        W("gpsimd", "s_dev", sch.val("s_dev"))
        E("gpsimd", lambda s: s.dma_start(out=dbg["y"], in_=y_sb[:]), inc="s_dbg", inck=16)


    free(ffnw_cms[:])

    W("sync", "s_dev", sch.val("s_dev"))

    junk3 = mk_sbF("junk3A", [128, D], BF16)
    gb_row = mk_sbF("gb_row", [1, 3 * D], F32)
    gam_bc = mk_sbF("gam_bc", [128, D], BF16)
    bet_bc = mk_sbF("bet_bc", [128, D], BF16)
    bo_bc = mk_sbF("bo_bc", [128, D], F32)
    yn_t = [mk_sbF(f"yn{i}", [128, D], BF16) for i in range(2)]
    g_sb = mk_sbF("g_sb", [128, NTO * D], BF16)

    E("sync", lambda s: s.dma_start(out=gb_row[0:1, 0:D], in_=gamma_d), inc="s_gdma", inck=16)
    E("sync", lambda s: s.dma_start(out=gb_row[0:1, D:2 * D], in_=beta_d), inc="s_gdma", inck=16)
    E("sync", lambda s: s.dma_start(out=gb_row[0:1, 2 * D:3 * D], in_=bo_d), inc="s_gdma", inck=16)
    W("tensor", "s_gdma", sch.val("s_gdma"))
    W("tensor", "s_dev", sch.val("s_dev"))
    W("tensor", "s_msk", 1)
    for k in range(3):
        for i in range(4):
            E("tensor", lambda pe, k=k, i=i: pe.matmul(
                out=pY[4 + i][:],
                lhsT=ones_f32[0:1, :],
                rhs=gb_row[0:1, k * D + i * 512:k * D + (i + 1) * 512],
                start=True, stop=True))
        sch.tag("tensor", "s_gb")
        tgt = [gam_bc, bet_bc, bo_bc][k]
        W("vector", "s_gb", sch.val("s_gb"))
        for i in range(4):
            E("vector", lambda v, tgt=tgt, i=i: v.tensor_copy(
                out=tgt[:, i * 512:(i + 1) * 512], in_=pY[4 + i][:]))
        sch.tag("vector", "s_gb")
        W("tensor", "s_gb", sch.val("s_gb"))

    for t in range(NTO):
        yt = y_sb[:, t * D:(t + 1) * D]
        W("vector", "s_jA", sch.val("s_jA"))
        E("vector", lambda v, yt=yt, t=t: v.tensor_scalar(
            out=junk3[:], in0=yt, scalar1=1.0, scalar2=0.0, op0=ALU.mult, op1=ALU.add,
            accum_out=stc("sx", t)), inc="s_jA")
        W("vector", "s_jA", sch.val("s_jA"))
        E("vector", lambda v, yt=yt, t=t: v.scalar_tensor_tensor(
            out=junk3[:], in0=yt, scalar=1.0, in1=yt, op0=ALU.mult, op1=ALU.mult,
            accum_out=stc("sq", t)), inc="s_jA")
    E("vector", lambda v: v.drain())
    emit_ln_block(0, NTO, "s_lnst", "s_ost", "s_orstd", "s_lnr")
    for t in range(NTO):
        yt = y_sb[:, t * D:(t + 1) * D]
        W("vector", "s_lnr", sch.val("s_lnr"))
        if t >= 2:
            W("vector", "s_g", sch.val("s_g") - 1)
        E("vector", lambda v, yt=yt, t=t: v.tensor_scalar(
            out=yn_t[t % 2][:], in0=yt, scalar1=stc("negmu", t), scalar2=stc("rstd", t),
            op0=ALU.add, op1=ALU.mult), inc="s_jA")
        W("vector", "s_jA", sch.val("s_jA"))
        if t == 0:
            W("vector", "s_gb", sch.val("s_gb"))
        E("vector", lambda v, t=t: v.tensor_tensor(
            out=yn_t[t % 2][:], in0=yn_t[t % 2][:], in1=gam_bc[:], op=ALU.mult), inc="s_jA")
        W("vector", "s_jA", sch.val("s_jA"))
        E("vector", lambda v, t=t: v.tensor_tensor(
            out=yn_t[t % 2][:], in0=yn_t[t % 2][:], in1=bet_bc[:], op=ALU.add), inc="s_yn")
        W("scalar", "s_yn", sch.val("s_yn"))
        E("scalar", lambda s, t=t: s.activation(
            out=g_sb[:, t * D:(t + 1) * D], in_=yn_t[t % 2][:], func=AF.Silu), inc="s_g")

    if debug_outputs:
        W("gpsimd", "s_g", sch.val("s_g"))
        E("gpsimd", lambda s: s.dma_start(out=dbg["g"], in_=g_sb[:]), inc="s_dbg", inck=16)

    s_t = [mk_sbF(f"st{i}", [128, D], BF16) for i in range(2)]
    xo_t = [mk_sbF(f"xot{i}", [128, D], F32) for i in range(2)]
    o_t = [mk_sbF(f"ot{i}", [128, D], F32) for i in range(1)]

    W("sync", "s_coll", sch.val("s_coll"))
    for t in range(NTO):
        if t >= 2:
            W("sync", "s_fin", t - 1)
        E("sync", lambda s, t=t: s.dma_start(
            out=s_t[t % 2][:], in_=ps_recv[t * 128:(t + 1) * 128, :]), inc=f"s_sd{t % 2}", inck=16)
        E("sync", lambda s, t=t: s.dma_start(
            out=xo_t[t % 2][:], in_=xo_d[t * 128:(t + 1) * 128, :]), inc=f"s_xo{t % 2}", inck=16)
        W("vector", f"s_sd{t % 2}", 16 * (t // 2 + 1))
        W("vector", f"s_xo{t % 2}", 16 * (t // 2 + 1))
        W("vector", "s_g", sch.val("s_g"))
        if t >= 1:
            W("vector", "s_out", 16 * t)
        E("vector", lambda v, t=t: v.tensor_tensor(
            out=o_t[0][:], in0=s_t[t % 2][:], in1=bo_bc[:], op=ALU.add), inc="s_jA")
        W("vector", "s_jA", sch.val("s_jA"))
        E("vector", lambda v, t=t: v.tensor_tensor(
            out=o_t[0][:], in0=o_t[0][:], in1=g_sb[:, t * D:(t + 1) * D], op=ALU.mult), inc="s_jA")
        W("vector", "s_jA", sch.val("s_jA"))
        E("vector", lambda v, t=t: v.tensor_tensor(
            out=o_t[0][:], in0=o_t[0][:], in1=xo_t[t % 2][:], op=ALU.add), inc="s_fin")
        W("sync", "s_fin", sch.val("s_fin"))
        E("sync", lambda s, t=t: s.dma_start(
            out=out_d[t * 128:(t + 1) * 128, :], in_=o_t[0][:]), inc="s_out", inck=16)

    W("sync", "s_out", 16 * NTO)
    if debug_outputs:
        W("gpsimd", "s_dbg", sch.val("s_dbg"))

    with nc.Block() as block:
        @block.sync
        def _(eng):
            for f in sch.ops["sync"]:
                f(eng)

        @block.gpsimd
        def _(eng):
            for f in sch.ops["gpsimd"]:
                f(eng)

        @block.vector
        def _(eng):
            for f in sch.ops["vector"]:
                f(eng)

        @block.scalar
        def _(eng):
            for f in sch.ops["scalar"]:
                f(eng)

        @block.tensor
        def _(eng):
            for f in sch.ops["tensor"]:
                f(eng)

    close_all()
    return nc


def kernel(L0_attn_hidden_states, L0_attn_Wq, L0_attn_Wk, L0_attn_Wv, L0_attn_Wo,
           L0_attn_bo, L0_ffn_gate_w, L0_ffn_up_w, L0_ffn_down_w,
           L0_ffn_expert_weight, L0_ffn_ln_gamma, L0_ffn_ln_beta):
    S, D_ = 8192, 2048
    TOK_OWN = S // N_CORES
    x = np.ascontiguousarray(np.asarray(L0_attn_hidden_states, dtype=np.float32).reshape(S, D_))
    wq = np.asarray(L0_attn_Wq, dtype=np.float32)
    wk = np.asarray(L0_attn_Wk, dtype=np.float32)
    wv = np.asarray(L0_attn_Wv, dtype=np.float32)
    wo = np.asarray(L0_attn_Wo, dtype=np.float32)
    bo = np.ascontiguousarray(np.asarray(L0_attn_bo, dtype=np.float32).reshape(1, D_))
    gate = np.ascontiguousarray(np.asarray(L0_ffn_gate_w, dtype=np.float32))
    up = np.ascontiguousarray(np.asarray(L0_ffn_up_w, dtype=np.float32))
    down = np.ascontiguousarray(np.asarray(L0_ffn_down_w, dtype=np.float32))
    expw = np.ascontiguousarray(np.asarray(L0_ffn_expert_weight, dtype=np.float32))
    gamma = np.ascontiguousarray(np.asarray(L0_ffn_ln_gamma, dtype=np.float32).reshape(1, D_))
    beta = np.ascontiguousarray(np.asarray(L0_ffn_ln_beta, dtype=np.float32).reshape(1, D_))

    nc = build_program(S=S, TOK_OWN=TOK_OWN)
    in_maps = []
    for c in range(N_CORES):
        in_maps.append({
            "x": x,
            "x_own": np.ascontiguousarray(x[c * TOK_OWN:(c + 1) * TOK_OWN]),
            "wq": np.ascontiguousarray(wq[:, c * 256:(c + 1) * 256]),
            "wk": np.ascontiguousarray(wk[:, c * 256:(c + 1) * 256]),
            "wv": np.ascontiguousarray(wv[:, c * 256:(c + 1) * 256]),
            "wo": np.ascontiguousarray(wo[c * 256:(c + 1) * 256, :]),
            "bo": bo, "gate": gate, "up": up, "down": down, "expw": expw,
            "gamma": gamma, "beta": beta,
        })
    res = run_bass_kernel_spmd(nc, in_maps, core_ids=list(range(N_CORES)))
    out = np.concatenate([res.results[c]["out"] for c in range(N_CORES)], axis=0)
    return out.reshape(1, S, D_)


# version 11
# speedup vs baseline: 1.0287x; 1.0058x over previous
import numpy as np
import concourse.bass as bass
import concourse.mybir as mybir
from concourse.bass_utils import run_bass_kernel_spmd

F32 = mybir.dt.float32
BF16 = mybir.dt.bfloat16
AF = mybir.ActivationFunctionType
ALU = mybir.AluOpType
AX = mybir.AxisListType

N_CORES = 8
D = 2048
FC = D // 128
INTER = 8192
NEXP = 8
EPS = 1e-5
NH_LOC = 2
VST = 132

ENGINES = ("sync", "gpsimd", "vector", "scalar", "tensor")


class Sched:
    def __init__(self):
        self.ops = {e: [] for e in ENGINES}
        self.sems = {}
        self.counts = {}

    def wait(self, engine, sem, val):
        if val <= 0:
            return
        s = self.sems[sem]
        self.ops[engine].append(lambda eng, s=s, v=val: eng.wait_ge(s, v))

    def emit(self, engine, fn, inc=None, inck=1):
        if inc is None:
            self.ops[engine].append(fn)
            return None
        s = self.sems[inc]
        self.ops[engine].append(lambda eng, s=s, k=inck, f=fn: f(eng).then_inc(s, k))
        self.counts[inc] += inck
        return self.counts[inc]

    def val(self, sem):
        return self.counts[sem]

    def tag(self, engine, inc, inck=1):
        s = self.sems[inc]
        prev = self.ops[engine].pop()
        self.ops[engine].append(lambda eng, s=s, k=inck, f=prev: f(eng).then_inc(s, k))
        self.counts[inc] += inck
        return self.counts[inc]


def build_program(S=8192, TOK_OWN=1024, debug_outputs=False):
    NT = S // 128
    NB = S // 512
    NTO = TOK_OWN // 128
    NTB = TOK_OWN // 512
    NISL = INTER // 128

    nc = bass.Bass("TRN2", target_bir_lowering=False)

    x_d = nc.dram_tensor("x", [S, D], F32, kind="ExternalInput").ap()
    xo_d = nc.dram_tensor("x_own", [TOK_OWN, D], F32, kind="ExternalInput").ap()
    wq_d = nc.dram_tensor("wq", [D, 256], F32, kind="ExternalInput").ap()
    wk_d = nc.dram_tensor("wk", [D, 256], F32, kind="ExternalInput").ap()
    wv_d = nc.dram_tensor("wv", [D, 256], F32, kind="ExternalInput").ap()
    wo_d = nc.dram_tensor("wo", [256, D], F32, kind="ExternalInput").ap()
    bo_d = nc.dram_tensor("bo", [1, D], F32, kind="ExternalInput").ap()
    gate_d = nc.dram_tensor("gate", [D, NEXP], F32, kind="ExternalInput").ap()
    up_d = nc.dram_tensor("up", [D, INTER], F32, kind="ExternalInput").ap()
    down_d = nc.dram_tensor("down", [INTER, D], F32, kind="ExternalInput").ap()
    expw_d = nc.dram_tensor("expw", [D, D], F32, kind="ExternalInput").ap()
    gamma_d = nc.dram_tensor("gamma", [1, D], F32, kind="ExternalInput").ap()
    beta_d = nc.dram_tensor("beta", [1, D], F32, kind="ExternalInput").ap()
    out_d = nc.dram_tensor("out", [TOK_OWN, D], F32, kind="ExternalOutput").ap()
    ps_send = nc.dram_tensor("ps_send", [S, D], BF16).ap()
    ps_recv = nc.dram_tensor("ps_recv", [TOK_OWN, D], BF16).ap()
    upbf = nc.dram_tensor("upbf", [(INTER // 128) * 128, D], BF16).ap()
    downbf = nc.dram_tensor("downbf", [4 * (INTER // 512) * 128, D], BF16).ap()
    expbf = nc.dram_tensor("expbf", [4 * 128, FC * 512], BF16).ap()
    gatebf = nc.dram_tensor("gatebf", [128, FC * NEXP], BF16).ap()

    dbg = {}
    if debug_outputs:
        dbg["qT"] = nc.dram_tensor("dbg_qT", [128, NH_LOC * S], F32, kind="ExternalOutput").ap()
        dbg["kT"] = nc.dram_tensor("dbg_kT", [128, NH_LOC * S], F32, kind="ExternalOutput").ap()
        dbg["v"] = nc.dram_tensor("dbg_v", [128, NH_LOC * NT * VST], F32, kind="ExternalOutput").ap()
        dbg["ctx"] = nc.dram_tensor("dbg_ctx", [128, NH_LOC * S], F32, kind="ExternalOutput").ap()
        dbg["y"] = nc.dram_tensor("dbg_y", [128, NTO * D], F32, kind="ExternalOutput").ap()
        dbg["g"] = nc.dram_tensor("dbg_g", [128, NTO * D], F32, kind="ExternalOutput").ap()
        dbg["xnT"] = nc.dram_tensor("dbg_xnT", [128, FC * TOK_OWN], F32, kind="ExternalOutput").ap()

    sch = Sched()
    sem_names = [
        "s_x", "s_stats", "s_rstd", "s_xn", "s_ptr", "s_xnT",
        "s_pqk", "s_qkev", "s_pv", "s_vev", "s_wst", "s_msk",
        "s_S", "s_P", "s_Pm", "s_AV", "s_ctx",
        "s_ctxT", "s_ctxTe", "s_wops", "s_psst", "s_psdma", "s_coll",
        "s_gdma", "s_glog", "s_scl",
        "s_ownx", "s_ost", "s_orstd", "s_oxn", "s_optr", "s_oxnT",
        "s_exdma", "s_ey", "s_yev",
        "s_updma", "s_up", "s_hrel", "s_h2",
        "s_dwdma", "s_dwn", "s_dev",
        "s_gb", "s_lnst", "s_lnr", "s_yn", "s_g",
        "s_sdma", "s_xodma", "s_fin", "s_out", "s_dbg", "s_pcast", "s_jA", "s_jB",
        "s_x0", "s_x1", "s_x2", "s_ox0", "s_ox1", "s_ox2",
        "s_upd0", "s_upd1", "s_dwd0", "s_dwd1", "s_exd0", "s_exd1",
        "s_sd0", "s_sd1", "s_xo0", "s_xo1", "s_ps0", "s_ps1",
    ]

    open_cms = []

    def close_all():
        for cm in reversed(open_cms):
            cm.__exit__(None, None, None)

    def mk(cm):
        t = cm.__enter__()
        open_cms.append(cm)
        return t, cm

    def mk_sb(name, shape, dtype):
        return mk(nc.sbuf_tensor(name, shape, dtype))

    def mk_ps(name, shape, dtype):
        return mk(nc.psum_tensor(name, shape, dtype))

    def free(cms):
        for cm in reversed(cms):
            cm.__exit__(None, None, None)
            open_cms.remove(cm)

    for s in sem_names:
        h, _ = mk(nc.semaphore(s))
        sch.sems[s] = h
        sch.counts[s] = 0

    E = sch.emit
    W = sch.wait

    ident, _ = mk_sb("ident", [128, 128], BF16)
    trimask, _ = mk_sb("trimask", [128, 128], BF16)
    ones_f32, _ = mk_sb("ones_f32", [1, 128], F32)
    NTT = NT + NTO
    stats, _ = mk_sb("stats", [128, 8 * NTT], F32)
    xnTo, _ = mk_sb("xnTo", [128, FC * TOK_OWN], BF16)

    def stc(kind, t, w=1):
        i = {"sx": 0, "sq": 1, "negmu": 2, "m2e": 3, "musq": 4,
             "vare": 5, "lnv": 6, "rstd": 7}[kind]
        return stats[:, i * NTT + t:i * NTT + t + w]

    def p0(g):
        g.memset(ident[:], 1.0)
        g.drain()
        g.affine_select(out=ident[:], in_=ident[:], pattern=[[-1, 128]],
                        compare_op=ALU.is_equal, fill=0.0, base=0, channel_multiplier=1)
        g.memset(trimask[:], 1.0)
        g.drain()
        g.affine_select(out=trimask[:], in_=trimask[:], pattern=[[1, 128]],
                        compare_op=ALU.is_ge, fill=0.0, base=0, channel_multiplier=-1)
        return g.memset(ones_f32[:], 1.0)

    E("gpsimd", p0, inc="s_msk")

    def emit_ln_accum(xb, t, jk, wait_sem, wait_val):
        jk1, jk2 = jk
        W("vector", wait_sem, wait_val)
        W("vector", "s_jA", sch.val("s_jA"))
        E("vector", lambda v, xb=xb, t=t, jk1=jk1: v.tensor_scalar(
            out=jk1[:], in0=xb[:], scalar1=1.0, scalar2=0.0, op0=ALU.mult, op1=ALU.add,
            accum_out=stc("sx", t)), inc="s_jA")
        W("vector", "s_jB", sch.val("s_jB"))
        E("vector", lambda v, xb=xb, t=t, jk2=jk2: v.scalar_tensor_tensor(
            out=jk2[:], in0=xb[:], scalar=1.0, in1=xb[:], op0=ALU.mult, op1=ALU.mult,
            accum_out=stc("sq", t)), inc="s_jB")

    def emit_ln_block(t0, w, sem_stats, sem_musq, sem_vare, sem_rstd):
        E("vector", lambda v: v.drain())
        E("vector", lambda v, t0=t0, w=w: v.tensor_scalar(
            out=stc("negmu", t0, w), in0=stc("sx", t0, w), scalar1=-1.0 / D,
            scalar2=None, op0=ALU.mult))
        E("vector", lambda v, t0=t0, w=w: v.tensor_scalar(
            out=stc("m2e", t0, w), in0=stc("sq", t0, w), scalar1=1.0 / D, scalar2=EPS,
            op0=ALU.mult, op1=ALU.add), inc=sem_stats)
        W("scalar", sem_stats, sch.val(sem_stats))
        E("scalar", lambda s, t0=t0, w=w: s.activation(
            out=stc("musq", t0, w), in_=stc("negmu", t0, w), func=AF.Square), inc=sem_musq)
        W("vector", sem_musq, sch.val(sem_musq))
        E("vector", lambda v, t0=t0, w=w: v.tensor_tensor(
            out=stc("vare", t0, w), in0=stc("m2e", t0, w), in1=stc("musq", t0, w),
            op=ALU.subtract), inc=sem_vare)
        W("scalar", sem_vare, sch.val(sem_vare))
        E("scalar", lambda s, t0=t0, w=w: s.activation(
            out=stc("lnv", t0, w), in_=stc("vare", t0, w), func=AF.Ln))
        E("scalar", lambda s: s.drain())
        E("scalar", lambda s, t0=t0, w=w: s.activation(
            out=stc("rstd", t0, w), in_=stc("lnv", t0, w), func=AF.Exp, scale=-0.5),
          inc=sem_rstd)

    def emit_ln_norm(xb, t, sem_rstd, sem_xn):
        W("vector", sem_rstd, sch.val(sem_rstd))
        E("vector", lambda v, xb=xb, t=t: v.tensor_scalar(
            out=xb[:], in0=xb[:], scalar1=stc("negmu", t), scalar2=stc("rstd", t),
            op0=ALU.add, op1=ALU.mult), inc=sem_xn)

    def emit_transpose16(xb, t, banks, sem_ptr, sem_evac, dst_fn, wait_sem):
        W("tensor", wait_sem, sch.val(wait_sem))
        for half in range(2):
            bank = banks[half]
            if t >= 1:
                W("tensor", sem_evac, (t - 1) * 2 + half + 1)
            for j in range(8):
                c = half * 8 + j
                E("tensor", lambda pe, xb=xb, bank=bank, j=j, c=c: pe.transpose(
                    out=bank[:, j * 128:(j + 1) * 128], in_=xb[:, c * 128:(c + 1) * 128],
                    identity=ident[:]))
            sch.tag("tensor", sem_ptr)
        for half in range(2):
            W("vector", sem_ptr, sch.val(sem_ptr) - (1 - half))
            dst = dst_fn(half)
            src = banks[half][:].rearrange("p (c q) -> p c q", c=8)
            E("vector", lambda v, dst=dst, src=src: v.tensor_copy(out=dst, in_=src), inc=sem_evac)

    b_cms = []

    def mk_sbB(name, shape, dtype):
        t, cm = mk_sb(name, shape, dtype)
        b_cms.append(cm)
        return t

    qT = mk_sbB("qT", [128, NH_LOC * S], BF16)
    kT = mk_sbB("kT", [128, NH_LOC * S], BF16)
    vb = mk_sbB("vb", [128, NH_LOC * NT * VST], BF16)

    b1_cms = []

    def mk_sbB1(name, shape, dtype):
        t, cm = mk_sb(name, shape, dtype)
        b1_cms.append(cm)
        return t

    wqk_st = mk_sbB1("wqk_st", [128, FC * 512], BF16)
    wv_st = mk_sbB1("wv_st", [128, FC * 256], BF16)
    xbf = [mk_sbB1(f"xbf{i}", [128, D], BF16) for i in range(6)]
    junk = (mk_sbB1("junkbA", [128, D], BF16), mk_sbB1("junkbB", [128, D], BF16))
    xnT = [mk_sbB1("xnT0", [128, FC * 512], BF16)]

    ps1_cms = []

    def mk_psS1(name, shape, dtype):
        t, cm = mk_ps(name, shape, dtype)
        ps1_cms.append(cm)
        return t

    ptrb = [mk_psS1(f"ptrb{i}", [128, 1024], BF16) for i in range(2)]
    pqk = [mk_psS1(f"pqk{i}", [128, 512], F32) for i in range(4)]
    pv = [mk_psS1(f"pv{i}", [128, 256], F32) for i in range(2)]

    E("vector", lambda v: v.memset(vb[:], 1.0))
    E("vector", lambda v: v.drain())

    def wdma(fn):
        E("gpsimd", fn, inc="s_wst", inck=16)

    wqk_v = wqk_st[:].rearrange("p (c z) -> p c z", c=FC)
    for part in range(4):
        lo, hi = part * 4, part * 4 + 4
        wdma(lambda g, lo=lo, hi=hi: g.dma_start(
            out=wqk_v[:, lo:hi, 0:256],
            in_=wq_d.rearrange("(c p) q -> p c q", p=128)[:, lo:hi, :]))
        wdma(lambda g, lo=lo, hi=hi: g.dma_start(
            out=wqk_v[:, lo:hi, 256:512],
            in_=wk_d.rearrange("(c p) q -> p c q", p=128)[:, lo:hi, :]))
        wdma(lambda g, lo=lo, hi=hi: g.dma_start(
            out=wv_st[:].rearrange("p (c q) -> p c q", c=FC)[:, lo:hi, :],
            in_=wv_d.rearrange("(c p) q -> p c q", p=128)[:, lo:hi, :]))
    WST_ALL = sch.val("s_wst")

    XSLOT = ["s_x0", "s_x1", "s_x2", "s_ox0", "s_ox1", "s_ox2"]
    for t in range(NT + NTO):
        b, ti = t // 4, t % 4
        xb = xbf[t % 6]
        if t >= 6:
            W("gpsimd", "s_ptr", 2 * (t - 5))
        if t < NT:
            E("gpsimd", lambda g, xb=xb, t=t: g.dma_start(
                out=xb[:], in_=x_d[t * 128:(t + 1) * 128, :]), inc=XSLOT[t % 6], inck=16)
        else:
            to = t - NT
            E("gpsimd", lambda g, xb=xb, to=to: g.dma_start(
                out=xb[:], in_=xo_d[to * 128:(to + 1) * 128, :]), inc=XSLOT[t % 6], inck=16)
        emit_ln_accum(xb, t, junk, XSLOT[t % 6], 16 * (t // 6 + 1))
        if t == 0:
            W("tensor", "s_msk", 1)
            W("tensor", "s_wst", WST_ALL)
        if ti != 3:
            continue
        emit_ln_block(t - 3, 4, "s_stats", "s_ost", "s_orstd", "s_rstd")
        for tt in range(4):
            emit_ln_norm(xbf[(t - 3 + tt) % 6], t - 3 + tt, "s_rstd", "s_xn")
        xt = xnT[0]
        for tt in range(4):
            tg = t - 3 + tt
            if tg < NT:
                dst_fn = (lambda half, xt=xt, tt=tt: xt[:].rearrange("p (c q) -> p c q", c=FC)[
                    :, half * 8:(half + 1) * 8, tt * 128:(tt + 1) * 128])
            else:
                to = tg - NT
                dst_fn = (lambda half, to=to: xnTo[:].rearrange("p (c q) -> p c q", c=FC)[
                    :, half * 8:(half + 1) * 8, to * 128:(to + 1) * 128])
            emit_transpose16(xbf[tg % 6], tg, ptrb, "s_ptr", "s_xnT", dst_fn, "s_xn")

        if t < NT:
            W("tensor", "s_xnT", sch.val("s_xnT"))
            if b >= 1:
                W("tensor", "s_qkev", b)
            for sl in range(4):
                for c in range(FC):
                    E("tensor", lambda pe, sl=sl, c=c, xt=xt: pe.matmul(
                        out=pqk[sl][:],
                        lhsT=wqk_st[:, c * 512 + sl * 128:c * 512 + (sl + 1) * 128],
                        rhs=xt[:, c * 512:(c + 1) * 512],
                        start=(c == 0), stop=(c == FC - 1)))
            sch.tag("tensor", "s_pqk")
            for tt in range(4):
                kt_g = b * 4 + tt
                if kt_g >= 2:
                    W("tensor", "s_vev", kt_g - 1)
                for c in range(FC):
                    E("tensor", lambda pe, tt=tt, c=c, xt=xt: pe.matmul(
                        out=pv[tt % 2][:],
                        lhsT=xt[:, c * 512 + tt * 128:c * 512 + (tt + 1) * 128],
                        rhs=wv_st[:, c * 256:(c + 1) * 256],
                        start=(c == 0), stop=(c == FC - 1)))
                sch.tag("tensor", "s_pv")
            W("vector", "s_pqk", sch.val("s_pqk"))
            for sl in range(4):
                tgt = qT if sl < 2 else kT
                h = sl % 2
                E("vector", lambda v, sl=sl, tgt=tgt, h=h, b=b: v.tensor_copy(
                    out=tgt[:, h * S + b * 512:h * S + (b + 1) * 512], in_=pqk[sl][:]))
            sch.tag("vector", "s_qkev")
            for tt in range(4):
                kt_g = b * 4 + tt
                W("vector", "s_pv", sch.val("s_pv") - (3 - tt))
                for h in range(NH_LOC):
                    E("vector", lambda v, tt=tt, h=h, kt_g=kt_g: v.tensor_copy(
                        out=vb[:, h * NT * VST + kt_g * VST:h * NT * VST + kt_g * VST + 128],
                        in_=pv[tt % 2][:, h * 128:(h + 1) * 128]))
                sch.tag("vector", "s_vev")

    P1_XNT_ALL = sch.val("s_xnT")

    if debug_outputs:
        W("gpsimd", "s_qkev", sch.val("s_qkev"))
        W("gpsimd", "s_vev", sch.val("s_vev"))
        E("gpsimd", lambda s: s.dma_start(out=dbg["qT"], in_=qT[:]), inc="s_dbg", inck=16)
        E("gpsimd", lambda s: s.dma_start(out=dbg["kT"], in_=kT[:]), inc="s_dbg", inck=16)
        E("gpsimd", lambda s: s.dma_start(out=dbg["v"], in_=vb[:]), inc="s_dbg", inck=16)

    free(ps1_cms[:])
    free(b1_cms[:])

    b2_cms = []

    def mk_sbB2(name, shape, dtype):
        t, cm = mk_sb(name, shape, dtype)
        b2_cms.append(cm)
        return t

    ctxb = mk_sbB2("ctxb", [128, NH_LOC * S], BF16)
    wo_st = mk_sbB2("wo_st", [128, 2 * D], BF16)
    W("gpsimd", "s_vev", sch.val("s_vev"))
    W("gpsimd", "s_pv", sch.val("s_pv"))
    W("gpsimd", "s_xnT", sch.val("s_xnT"))
    E("gpsimd", lambda g: g.dma_start(
        out=wo_st[:].rearrange("p (c q) -> p c q", c=2),
        in_=wo_d.rearrange("(c p) q -> p c q", p=128)), inc="s_wst", inck=16)
    upbf_v = upbf.rearrange("(s p) x -> s p x", p=128)
    for isl in range(INTER // 128):
        E("gpsimd", lambda g, isl=isl: g.dma_start(
            out=upbf_v[isl].rearrange("p (c i) -> p c i", i=128),
            in_=up_d.rearrange("(c p) (s i) -> s p c i", p=128, i=128)[isl]),
          inc="s_pcast", inck=16)
    dnbf_v = downbf.rearrange("(f g p) x -> f g p x", g=INTER // 512, p=128)
    for fb in range(4):
        for grp in range(INTER // 512):
            E("gpsimd", lambda g, fb=fb, grp=grp: g.dma_start(
                out=dnbf_v[fb, grp].rearrange("p (c q) -> p c q", c=4),
                in_=down_d.rearrange("(g c p) q -> g p c q", p=128, c=4)[grp][:, :,
                                                                             fb * 512:(fb + 1) * 512]),
              inc="s_pcast", inck=16)
    exbf_v = expbf.rearrange("(f p) x -> f p x", p=128)
    for fb in range(4):
        for half in range(2):
            E("gpsimd", lambda g, fb=fb, half=half: g.dma_start(
                out=exbf_v[fb].rearrange("p (c q) -> p c q", c=16)[:, half * 8:(half + 1) * 8, :],
                in_=expw_d.rearrange("(c p) q -> p c q", p=128)[
                    :, half * 8:(half + 1) * 8, fb * 512:(fb + 1) * 512]),
              inc="s_pcast", inck=16)
    E("gpsimd", lambda g: g.dma_start(
        out=gatebf.rearrange("p (c e) -> p c e", c=FC),
        in_=gate_d.rearrange("(c p) e -> p c e", p=128)), inc="s_pcast", inck=16)
    PCAST_ALL = sch.val("s_pcast")
    pTb = [mk_sbB2(f"pT{i}", [128, 1536], BF16) for i in range(2)]
    pdiag = mk_sbB2("pdiag", [128, 4 * 128], BF16)
    rc4 = mk_sbB2("rc4", [128, 4], F32)
    ctxT_sb = [mk_sbB2(f"ctxT{i}", [128, 256], BF16) for i in range(2)]
    ps_stage = [mk_sbB2(f"psst{i}", [128, D], BF16) for i in range(2)]

    ps2_cms = []
    pS = []
    for i in range(2):
        t_, cm = mk_ps(f"pS{i}", [128, 1536], F32)
        ps2_cms.append(cm)
        pS.append(t_)
    pC = []
    for i in range(2):
        t_, cm = mk_ps(f"pC{i}", [128, 512], F32)
        ps2_cms.append(cm)
        pC.append(t_)

    def ctx_ps(qt):
        return pC[qt // 2][:, (qt % 2) * 256:(qt % 2) * 256 + 129]

    W("tensor", "s_qkev", sch.val("s_qkev"))
    W("tensor", "s_vev", sch.val("s_vev"))
    W("tensor", "s_xnT", P1_XNT_ALL)

    for h in range(NH_LOC):
        hq, hv = h * S, h * NT * VST
        for qb in range(NB):
            nk = 4 * (qb + 1)
            ngroups = (nk + 2) // 3
            p_cnt = {}
            av_cnt = {}
            pm_cnt = {}

            def emit_S(g):
                j0, j1 = 3 * g, min(3 * g + 3, nk)
                bank = pS[g % 2]
                if g >= 2:
                    W("tensor", "s_P", p_cnt[g - 2])
                for j in range(j0, j1):
                    E("tensor", lambda pe, j=j, j0=j0, bank=bank, hq=hq, qb=qb: pe.matmul(
                        out=bank[:, (j - j0) * 512:(j - j0 + 1) * 512],
                        lhsT=kT[:, hq + j * 128:hq + (j + 1) * 128],
                        rhs=qT[:, hq + qb * 512:hq + (qb + 1) * 512],
                        start=True, stop=True))
                sch.tag("tensor", "s_S")

            def emit_exp(g):
                j0, j1 = 3 * g, min(3 * g + 3, nk)
                n = (j1 - j0) * 512
                W("scalar", "s_S", sch.val("s_S"))
                if g >= 2:
                    W("scalar", "s_AV", av_cnt[g - 2])
                E("scalar", lambda s, g=g, n=n: s.activation(
                    out=pTb[g % 2][:, 0:n], in_=pS[g % 2][:, 0:n], func=AF.Exp), inc="s_P")
                p_cnt[g] = sch.val("s_P")
                dt_ = [(j, j - 4 * qb) for j in range(j0, j1) if j >= 4 * qb]
                for (j, qt) in dt_:
                    W("vector", "s_P", p_cnt[g])
                    E("vector", lambda v, g=g, j=j, j0=j0, qt=qt: v.tensor_tensor(
                        out=pdiag[:, qt * 128:(qt + 1) * 128],
                        in0=pTb[g % 2][:, (j - j0) * 512 + qt * 128:(j - j0) * 512 + (qt + 1) * 128],
                        in1=trimask[:], op=ALU.mult), inc="s_Pm")
                pm_cnt[g] = sch.val("s_Pm")

            def emit_AV(g):
                j0, j1 = 3 * g, min(3 * g + 3, nk)
                if g == 0:
                    W("tensor", "s_ctx", sch.val("s_ctx"))
                W("tensor", "s_P", p_cnt[g])
                if pm_cnt.get(g, 0):
                    W("tensor", "s_Pm", pm_cnt[g])
                for j in range(j0, j1):
                    for qt in range(4):
                        jmax = 4 * qb + qt
                        if j > jmax:
                            continue
                        if j == jmax:
                            lhs = pdiag[:, qt * 128:(qt + 1) * 128]
                        else:
                            lhs = pTb[g % 2][:, (j - j0) * 512 + qt * 128:(j - j0) * 512 + (qt + 1) * 128]
                        E("tensor", lambda pe, lhs=lhs, qt=qt, j=j, jmax=jmax, hv=hv: pe.matmul(
                            out=ctx_ps(qt), lhsT=lhs,
                            rhs=vb[:, hv + j * VST:hv + j * VST + 129],
                            start=(j == 0 and qt % 2 == 0),
                            stop=(j == jmax and qt % 2 == 1)))
                sch.tag("tensor", "s_AV")
                av_cnt[g] = sch.val("s_AV")

            for g in range(ngroups):
                emit_S(g)
                emit_exp(g)
                if g >= 1:
                    emit_AV(g - 1)
            emit_AV(ngroups - 1)

            W("vector", "s_AV", av_cnt[ngroups - 1])
            for qt in range(4):
                E("vector", lambda v, qt=qt: v.reciprocal(
                    out=rc4[:, qt:qt + 1], in_=ctx_ps(qt)[:, 128:129]))
            E("vector", lambda v: v.drain())
            for qt in range(4):
                tix = qb * 4 + qt
                E("vector", lambda v, qt=qt, tix=tix, hq=hq: v.tensor_scalar(
                    out=ctxb[:, hq + tix * 128:hq + (tix + 1) * 128],
                    in0=ctx_ps(qt)[:, 0:128],
                    scalar1=rc4[:, qt:qt + 1], scalar2=None, op0=ALU.mult))
            sch.tag("vector", "s_ctx")

    if debug_outputs:
        W("gpsimd", "s_ctx", sch.val("s_ctx"))
        E("gpsimd", lambda s: s.dma_start(out=dbg["ctx"], in_=ctxb[:]), inc="s_dbg", inck=16)

    free(ps2_cms[:])

    ps25_cms = []
    pCt = []
    for i in range(2):
        t_, cm = mk_ps(f"pCt{i}", [128, 256], BF16)
        ps25_cms.append(cm)
        pCt.append(t_)
    pWo = []
    for i in range(4):
        t_, cm = mk_ps(f"pWo{i}", [128, 512], F32)
        ps25_cms.append(cm)
        pWo.append(t_)

    W("tensor", "s_wst", sch.val("s_wst"))
    for t in range(NT):
        if t >= 2:
            W("tensor", "s_ctxTe", t - 1)
        for h in range(NH_LOC):
            E("tensor", lambda pe, t=t, h=h: pe.transpose(
                out=pCt[t % 2][:, h * 128:(h + 1) * 128],
                in_=ctxb[:, h * S + t * 128:h * S + (t + 1) * 128],
                identity=ident[:]))
        sch.tag("tensor", "s_ctxT")
        W("vector", "s_ctxT", sch.val("s_ctxT"))
        E("vector", lambda v, t=t: v.tensor_copy(
            out=ctxT_sb[t % 2][:], in_=pCt[t % 2][:]), inc="s_ctxTe")
        W("tensor", "s_ctxTe", sch.val("s_ctxTe"))
        for fb in range(4):
            if t >= 1:
                W("tensor", "s_psst", (t - 1) * 4 + fb + 1)
            for ch in range(2):
                E("tensor", lambda pe, t=t, fb=fb, ch=ch: pe.matmul(
                    out=pWo[fb][:],
                    lhsT=ctxT_sb[t % 2][:, ch * 128:(ch + 1) * 128],
                    rhs=wo_st[:, ch * D + fb * 512:ch * D + (fb + 1) * 512],
                    start=(ch == 0), stop=(ch == 1)))
            sch.tag("tensor", "s_wops")
            W("scalar", "s_wops", sch.val("s_wops"))
            if fb == 0 and t >= 2:
                W("scalar", f"s_ps{t % 2}", 16 * (t // 2))
            E("scalar", lambda s, t=t, fb=fb: s.copy(
                out=ps_stage[t % 2][:, fb * 512:(fb + 1) * 512], in_=pWo[fb][:]), inc="s_psst")
        W("sync", "s_psst", sch.val("s_psst"))
        E("sync", lambda s, t=t: s.dma_start(
            out=ps_send[t * 128:(t + 1) * 128, :], in_=ps_stage[t % 2][:]), inc=f"s_ps{t % 2}", inck=16)

    W("gpsimd", "s_ps0", 16 * (NT // 2))
    W("gpsimd", "s_ps1", 16 * (NT // 2))
    E("gpsimd", lambda g: g.collective_compute(
        "ReduceScatter", ALU.add, replica_groups=[list(range(N_CORES))],
        ins=[ps_send.opt()], outs=[ps_recv.opt()]), inc="s_coll")

    free(ps25_cms[:])
    free(b2_cms[:])
    free(b_cms[:])

    f_cms = []

    def mk_sbF(name, shape, dtype):
        t, cm = mk_sb(name, shape, dtype)
        f_cms.append(cm)
        return t

    gate_st = mk_sbF("gate_st", [128, FC * NEXP], BF16)
    y_sb = mk_sbF("y_sb", [128, NTO * D], BF16)
    logits = mk_sbF("logits", [128, NTO * NEXP], F32)
    l_tmp = mk_sbF("l_tmp", [128, NEXP], F32)
    max12 = mk_sbF("max12", [128, 2], F32)
    rscl = mk_sbF("rscl", [128, NTO], F32)

    W("sync", "s_wops", sch.val("s_wops"))
    W("sync", "s_ps0", 16 * (NT // 2))
    W("sync", "s_ps1", 16 * (NT // 2))
    W("sync", "s_pcast", PCAST_ALL)
    if debug_outputs:
        W("sync", "s_dbg", sch.val("s_dbg"))
    E("sync", lambda s: s.dma_start(out=gate_st[:], in_=gatebf), inc="s_gdma", inck=16)

    if debug_outputs:
        W("gpsimd", "s_xnT", P1_XNT_ALL)
        E("gpsimd", lambda s: s.dma_start(out=dbg["xnT"], in_=xnTo[:]), inc="s_dbg", inck=16)

    psF_cms = []
    pY = []
    for i in range(8):
        t_, cm = mk_ps(f"pY{i}", [128, 512], F32)
        psF_cms.append(cm)
        pY.append(t_)

    W("tensor", "s_xnT", P1_XNT_ALL)
    W("tensor", "s_gdma", sch.val("s_gdma"))
    for t in range(NTO):
        if t >= 1:
            W("tensor", "s_glog", t)
        for c in range(FC):
            E("tensor", lambda pe, t=t, c=c: pe.matmul(
                out=pY[0][:, 0:NEXP],
                lhsT=xnTo[:, c * TOK_OWN + t * 128:c * TOK_OWN + (t + 1) * 128],
                rhs=gate_st[:, c * NEXP:(c + 1) * NEXP],
                start=(c == 0), stop=(c == FC - 1)))
        sch.tag("tensor", "s_S")
        W("vector", "s_S", sch.val("s_S"))
        E("vector", lambda v, t=t: v.tensor_copy(
            out=logits[:, t * NEXP:(t + 1) * NEXP], in_=pY[0][:, 0:NEXP]), inc="s_glog")

    for t in range(NTO):
        lg = logits[:, t * NEXP:(t + 1) * NEXP]
        E("vector", lambda v, lg=lg: v.tensor_reduce(
            out=max12[:, 0:1], in_=lg, axis=AX.X, op=ALU.max))
        E("vector", lambda v: v.drain())
        E("vector", lambda v, lg=lg: v.tensor_scalar(
            out=l_tmp[:], in0=lg, scalar1=max12[:, 0:1], scalar2=None, op0=ALU.is_equal))
        E("vector", lambda v: v.drain())
        E("vector", lambda v, lg=lg: v.scalar_tensor_tensor(
            out=l_tmp[:], in0=l_tmp[:], scalar=-3.0e38, in1=lg, op0=ALU.mult, op1=ALU.add))
        E("vector", lambda v: v.drain())
        E("vector", lambda v: v.tensor_reduce(
            out=max12[:, 1:2], in_=l_tmp[:], axis=AX.X, op=ALU.max))
        E("vector", lambda v: v.drain())
        E("vector", lambda v, t=t: v.tensor_tensor(
            out=rscl[:, t:t + 1], in0=max12[:, 0:1], in1=max12[:, 1:2], op=ALU.add))
        E("vector", lambda v: v.drain())
    sch.tag("vector", "s_scl")

    ex_cms = []

    def mk_sbEX(name, shape, dtype):
        t, cm = mk_sb(name, shape, dtype)
        ex_cms.append(cm)
        return t

    exslice = [mk_sbEX(f"exsl{i}", [128, FC * 512], BF16) for i in range(2)]
    for fb in range(4):
        if fb >= 2:
            W("sync", "s_ey", fb - 1)
        E("sync", lambda s, fb=fb: s.dma_start(
            out=exslice[fb % 2][:], in_=expbf[fb * 128:(fb + 1) * 128, :]),
          inc=f"s_exd{fb % 2}", inck=16)
        W("tensor", f"s_exd{fb % 2}", 16 * (fb // 2 + 1))
        if fb == 0:
            W("tensor", "s_glog", NTO)
        if fb >= 1:
            W("tensor", "s_yev", fb)
        for c in range(FC):
            for T in range(NTO):
                E("tensor", lambda pe, c=c, T=T, fb=fb: pe.matmul(
                    out=pY[T][:],
                    lhsT=xnTo[:, c * TOK_OWN + T * 128:c * TOK_OWN + (T + 1) * 128],
                    rhs=exslice[fb % 2][:, c * 512:(c + 1) * 512],
                    start=(c == 0), stop=(c == FC - 1)))
        sch.tag("tensor", "s_ey")
        W("vector", "s_ey", sch.val("s_ey"))
        if fb == 0:
            W("vector", "s_scl", 1)
        for T in range(NTO):
            E("vector", lambda v, T=T, fb=fb: v.tensor_scalar(
                out=y_sb[:, T * D + fb * 512:T * D + (fb + 1) * 512],
                in0=pY[T][:], scalar1=rscl[:, T:T + 1], scalar2=None, op0=ALU.mult))
        sch.tag("vector", "s_yev")

    free(ex_cms[:])
    ffnw_cms = []

    def mk_sbFW(name, shape, dtype):
        t, cm = mk_sb(name, shape, dtype)
        ffnw_cms.append(cm)
        return t

    h2T = mk_sbFW("h2T", [128, NISL * TOK_OWN], BF16)
    up_cms = []

    def mk_sbUP(name, shape, dtype):
        t, cm = mk_sb(name, shape, dtype)
        up_cms.append(cm)
        return t

    upst = [mk_sbUP(f"upst{i}", [128, D], BF16) for i in range(2)]
    hrel = [mk_sbUP(f"hrel{i}", [128, TOK_OWN], BF16) for i in range(2)]
    pH = [pY[0], pY[1]]

    for isl in range(NISL):
        ub = upst[isl % 2]
        if isl == 0:
            W("sync", "s_ey", 4)
        if isl >= 2:
            W("sync", "s_up", isl - 1)
        E("sync", lambda s, isl=isl, ub=ub: s.dma_start(
            out=ub[:], in_=upbf[isl * 128:(isl + 1) * 128, :]),
          inc=f"s_upd{isl % 2}", inck=16)
        W("tensor", f"s_upd{isl % 2}", 16 * (isl // 2 + 1))
        if isl == 0:
            W("tensor", "s_yev", 4)
        if isl >= 1:
            W("tensor", "s_hrel", isl)
        for c in range(FC):
            for tb in range(NTB):
                E("tensor", lambda pe, c=c, tb=tb, ub=ub: pe.matmul(
                    out=pH[tb][:],
                    lhsT=ub[:, c * 128:(c + 1) * 128],
                    rhs=xnTo[:, c * TOK_OWN + tb * 512:c * TOK_OWN + (tb + 1) * 512],
                    start=(c == 0), stop=(c == FC - 1)))
        sch.tag("tensor", "s_up")
        W("scalar", "s_up", sch.val("s_up"))
        if isl >= 2:
            W("scalar", "s_h2", isl - 1)
        for tb in range(NTB):
            E("scalar", lambda s, tb=tb, isl=isl: s.activation(
                out=hrel[isl % 2][:, tb * 512:(tb + 1) * 512], in_=pH[tb][:], func=AF.Relu))
        sch.tag("scalar", "s_hrel")
        W("vector", "s_hrel", sch.val("s_hrel"))
        E("vector", lambda v, isl=isl: v.tensor_tensor(
            out=h2T[:, isl * TOK_OWN:(isl + 1) * TOK_OWN],
            in0=hrel[isl % 2][:], in1=hrel[isl % 2][:], op=ALU.mult), inc="s_h2")

    free(up_cms[:])
    dwst = [mk_sbFW(f"dwst{i}", [128, 4 * 512], BF16) for i in range(2)]
    NGRP = NISL // 4
    W("tensor", "s_h2", sch.val("s_h2"))
    for fb in range(4):
        for grp in range(NGRP):
            gidx = fb * NGRP + grp
            if gidx < 2:
                W("sync", "s_up", NISL)
            if gidx >= 2:
                W("sync", "s_dwn", gidx - 1)
            E("sync", lambda s, fb=fb, grp=grp, gidx=gidx: s.dma_start(
                out=dwst[gidx % 2][:],
                in_=downbf[(fb * NGRP + grp) * 128:(fb * NGRP + grp + 1) * 128, :]),
              inc=f"s_dwd{gidx % 2}", inck=16)
            W("tensor", f"s_dwd{gidx % 2}", 16 * (gidx // 2 + 1))
            if grp == 0 and fb >= 1:
                W("tensor", "s_dev", fb)
            for i8 in range(4):
                isl = grp * 4 + i8
                for T in range(NTO):
                    E("tensor", lambda pe, gidx=gidx, i8=i8, isl=isl, T=T: pe.matmul(
                        out=pY[T][:],
                        lhsT=h2T[:, isl * TOK_OWN + T * 128:isl * TOK_OWN + (T + 1) * 128],
                        rhs=dwst[gidx % 2][:, i8 * 512:(i8 + 1) * 512],
                        start=(isl == 0), stop=(isl == NISL - 1)))
            sch.tag("tensor", "s_dwn")
        W("vector", "s_dwn", sch.val("s_dwn"))
        for T in range(NTO):
            yap = y_sb[:, T * D + fb * 512:T * D + (fb + 1) * 512]
            E("vector", lambda v, T=T, yap=yap: v.tensor_tensor(
                out=yap, in0=pY[T][:], in1=yap, op=ALU.add))
        sch.tag("vector", "s_dev")

    if debug_outputs:
        W("gpsimd", "s_dev", sch.val("s_dev"))
        E("gpsimd", lambda s: s.dma_start(out=dbg["y"], in_=y_sb[:]), inc="s_dbg", inck=16)


    free(ffnw_cms[:])

    W("sync", "s_dev", sch.val("s_dev"))

    junk3 = mk_sbF("junk3A", [128, D], BF16)
    gb_row = mk_sbF("gb_row", [1, 3 * D], F32)
    gam_bc = mk_sbF("gam_bc", [128, D], BF16)
    bet_bc = mk_sbF("bet_bc", [128, D], BF16)
    bo_bc = mk_sbF("bo_bc", [128, D], F32)
    yn_t = [mk_sbF(f"yn{i}", [128, D], BF16) for i in range(2)]
    g_sb = mk_sbF("g_sb", [128, NTO * D], BF16)

    E("sync", lambda s: s.dma_start(out=gb_row[0:1, 0:D], in_=gamma_d), inc="s_gdma", inck=16)
    E("sync", lambda s: s.dma_start(out=gb_row[0:1, D:2 * D], in_=beta_d), inc="s_gdma", inck=16)
    E("sync", lambda s: s.dma_start(out=gb_row[0:1, 2 * D:3 * D], in_=bo_d), inc="s_gdma", inck=16)
    W("tensor", "s_gdma", sch.val("s_gdma"))
    W("tensor", "s_dev", sch.val("s_dev"))
    W("tensor", "s_msk", 1)
    for k in range(3):
        for i in range(4):
            E("tensor", lambda pe, k=k, i=i: pe.matmul(
                out=pY[4 + i][:],
                lhsT=ones_f32[0:1, :],
                rhs=gb_row[0:1, k * D + i * 512:k * D + (i + 1) * 512],
                start=True, stop=True))
        sch.tag("tensor", "s_gb")
        tgt = [gam_bc, bet_bc, bo_bc][k]
        W("vector", "s_gb", sch.val("s_gb"))
        for i in range(4):
            E("vector", lambda v, tgt=tgt, i=i: v.tensor_copy(
                out=tgt[:, i * 512:(i + 1) * 512], in_=pY[4 + i][:]))
        sch.tag("vector", "s_gb")
        W("tensor", "s_gb", sch.val("s_gb"))

    for t in range(NTO):
        yt = y_sb[:, t * D:(t + 1) * D]
        W("vector", "s_jA", sch.val("s_jA"))
        E("vector", lambda v, yt=yt, t=t: v.tensor_scalar(
            out=junk3[:], in0=yt, scalar1=1.0, scalar2=0.0, op0=ALU.mult, op1=ALU.add,
            accum_out=stc("sx", t)), inc="s_jA")
        W("vector", "s_jA", sch.val("s_jA"))
        E("vector", lambda v, yt=yt, t=t: v.scalar_tensor_tensor(
            out=junk3[:], in0=yt, scalar=1.0, in1=yt, op0=ALU.mult, op1=ALU.mult,
            accum_out=stc("sq", t)), inc="s_jA")
    E("vector", lambda v: v.drain())
    emit_ln_block(0, NTO, "s_lnst", "s_ost", "s_orstd", "s_lnr")
    for t in range(NTO):
        yt = y_sb[:, t * D:(t + 1) * D]
        W("vector", "s_lnr", sch.val("s_lnr"))
        if t >= 2:
            W("vector", "s_g", sch.val("s_g") - 1)
        E("vector", lambda v, yt=yt, t=t: v.tensor_scalar(
            out=yn_t[t % 2][:], in0=yt, scalar1=stc("negmu", t), scalar2=stc("rstd", t),
            op0=ALU.add, op1=ALU.mult), inc="s_jA")
        W("vector", "s_jA", sch.val("s_jA"))
        if t == 0:
            W("vector", "s_gb", sch.val("s_gb"))
        E("vector", lambda v, t=t: v.tensor_tensor(
            out=yn_t[t % 2][:], in0=yn_t[t % 2][:], in1=gam_bc[:], op=ALU.mult), inc="s_jA")
        W("vector", "s_jA", sch.val("s_jA"))
        E("vector", lambda v, t=t: v.tensor_tensor(
            out=yn_t[t % 2][:], in0=yn_t[t % 2][:], in1=bet_bc[:], op=ALU.add), inc="s_yn")
        W("scalar", "s_yn", sch.val("s_yn"))
        E("scalar", lambda s, t=t: s.activation(
            out=g_sb[:, t * D:(t + 1) * D], in_=yn_t[t % 2][:], func=AF.Silu), inc="s_g")

    if debug_outputs:
        W("gpsimd", "s_g", sch.val("s_g"))
        E("gpsimd", lambda s: s.dma_start(out=dbg["g"], in_=g_sb[:]), inc="s_dbg", inck=16)

    s_t = [mk_sbF(f"st{i}", [128, D], BF16) for i in range(2)]
    xo_t = [mk_sbF(f"xot{i}", [128, D], F32) for i in range(2)]
    o_t = [mk_sbF(f"ot{i}", [128, D], F32) for i in range(1)]

    W("sync", "s_coll", sch.val("s_coll"))
    for t in range(NTO):
        if t >= 2:
            W("sync", "s_fin", t - 1)
        E("sync", lambda s, t=t: s.dma_start(
            out=s_t[t % 2][:], in_=ps_recv[t * 128:(t + 1) * 128, :]), inc=f"s_sd{t % 2}", inck=16)
        E("sync", lambda s, t=t: s.dma_start(
            out=xo_t[t % 2][:], in_=xo_d[t * 128:(t + 1) * 128, :]), inc=f"s_xo{t % 2}", inck=16)
        W("vector", f"s_sd{t % 2}", 16 * (t // 2 + 1))
        W("vector", f"s_xo{t % 2}", 16 * (t // 2 + 1))
        W("vector", "s_g", sch.val("s_g"))
        if t >= 1:
            W("vector", "s_out", 16 * t)
        E("vector", lambda v, t=t: v.tensor_tensor(
            out=o_t[0][:], in0=s_t[t % 2][:], in1=bo_bc[:], op=ALU.add), inc="s_jA")
        W("vector", "s_jA", sch.val("s_jA"))
        E("vector", lambda v, t=t: v.tensor_tensor(
            out=o_t[0][:], in0=o_t[0][:], in1=g_sb[:, t * D:(t + 1) * D], op=ALU.mult), inc="s_jA")
        W("vector", "s_jA", sch.val("s_jA"))
        E("vector", lambda v, t=t: v.tensor_tensor(
            out=o_t[0][:], in0=o_t[0][:], in1=xo_t[t % 2][:], op=ALU.add), inc="s_fin")
        W("sync", "s_fin", sch.val("s_fin"))
        E("sync", lambda s, t=t: s.dma_start(
            out=out_d[t * 128:(t + 1) * 128, :], in_=o_t[0][:]), inc="s_out", inck=16)

    W("sync", "s_out", 16 * NTO)
    if debug_outputs:
        W("gpsimd", "s_dbg", sch.val("s_dbg"))

    with nc.Block() as block:
        @block.sync
        def _(eng):
            for f in sch.ops["sync"]:
                f(eng)

        @block.gpsimd
        def _(eng):
            for f in sch.ops["gpsimd"]:
                f(eng)

        @block.vector
        def _(eng):
            for f in sch.ops["vector"]:
                f(eng)

        @block.scalar
        def _(eng):
            for f in sch.ops["scalar"]:
                f(eng)

        @block.tensor
        def _(eng):
            for f in sch.ops["tensor"]:
                f(eng)

    close_all()
    return nc


def kernel(L0_attn_hidden_states, L0_attn_Wq, L0_attn_Wk, L0_attn_Wv, L0_attn_Wo,
           L0_attn_bo, L0_ffn_gate_w, L0_ffn_up_w, L0_ffn_down_w,
           L0_ffn_expert_weight, L0_ffn_ln_gamma, L0_ffn_ln_beta):
    S, D_ = 8192, 2048
    TOK_OWN = S // N_CORES
    x = np.ascontiguousarray(np.asarray(L0_attn_hidden_states, dtype=np.float32).reshape(S, D_))
    wq = np.asarray(L0_attn_Wq, dtype=np.float32)
    wk = np.asarray(L0_attn_Wk, dtype=np.float32)
    wv = np.asarray(L0_attn_Wv, dtype=np.float32)
    wo = np.asarray(L0_attn_Wo, dtype=np.float32)
    bo = np.ascontiguousarray(np.asarray(L0_attn_bo, dtype=np.float32).reshape(1, D_))
    gate = np.ascontiguousarray(np.asarray(L0_ffn_gate_w, dtype=np.float32))
    up = np.ascontiguousarray(np.asarray(L0_ffn_up_w, dtype=np.float32))
    down = np.ascontiguousarray(np.asarray(L0_ffn_down_w, dtype=np.float32))
    expw = np.ascontiguousarray(np.asarray(L0_ffn_expert_weight, dtype=np.float32))
    gamma = np.ascontiguousarray(np.asarray(L0_ffn_ln_gamma, dtype=np.float32).reshape(1, D_))
    beta = np.ascontiguousarray(np.asarray(L0_ffn_ln_beta, dtype=np.float32).reshape(1, D_))

    nc = build_program(S=S, TOK_OWN=TOK_OWN)
    in_maps = []
    for c in range(N_CORES):
        in_maps.append({
            "x": x,
            "x_own": np.ascontiguousarray(x[c * TOK_OWN:(c + 1) * TOK_OWN]),
            "wq": np.ascontiguousarray(wq[:, c * 256:(c + 1) * 256]),
            "wk": np.ascontiguousarray(wk[:, c * 256:(c + 1) * 256]),
            "wv": np.ascontiguousarray(wv[:, c * 256:(c + 1) * 256]),
            "wo": np.ascontiguousarray(wo[c * 256:(c + 1) * 256, :]),
            "bo": bo, "gate": gate, "up": up, "down": down, "expw": expw,
            "gamma": gamma, "beta": beta,
        })
    res = run_bass_kernel_spmd(nc, in_maps, core_ids=list(range(N_CORES)))
    out = np.concatenate([res.results[c]["out"] for c in range(N_CORES)], axis=0)
    return out.reshape(1, S, D_)
